# version 36
# baseline (speedup 1.0000x reference)
import contextlib

import numpy as np
import ml_dtypes
import concourse.bass as bass
import concourse.mybir as mybir
from concourse.bass_utils import run_bass_kernel_spmd

F32 = mybir.dt.float32
BF16 = mybir.dt.bfloat16
AF = mybir.ActivationFunctionType
ALU = mybir.AluOpType
AX = mybir.AxisListType.X

P = 128
T = 4096
E = 2048
NO = 24
NEc = 16
H = 16
HKV = 4
D = 128
TCA = 512
SCALE = 1.0 / float(np.sqrt(128.0))
NEG = -1.0e30
EPS = 1.0e-5
NEXP = 12

_SERIAL = False
_RESCHED = True
_SAME_ENGINE_EDGES = True
_NO_SKIP = False
_DMA_EDGES = False
_DBG = False


class Sched:

    DEF_COST = {"tensor": 300.0, "vector": 500.0, "scalar": 500.0,
                "gpsimd": 600.0, "sync": 400.0}
    DMA_ISSUE = 220.0
    DMA_LAT = 1400.0

    @staticmethod
    def _eng_of(stream):
        if stream.startswith("gdma"):
            return "gpsimd"
        if stream.startswith("sdma"):
            return "sync"
        return stream

    class Op:
        __slots__ = ("eng", "stream", "idx", "fn", "dma", "deps", "inc",
                     "cum", "waits", "cost", "lat", "prio", "avail",
                     "seq", "succs", "nin")

        def __init__(self, eng, stream, idx, fn, dma, deps, cost, lat, seq):
            self.eng, self.stream, self.idx = eng, stream, idx
            self.fn, self.dma, self.deps = fn, dma, deps
            self.inc = dma
            self.cum = 0
            self.waits = []
            self.cost, self.lat, self.seq = cost, lat, seq
            self.prio = 0.0
            self.avail = 0.0
            self.succs = []
            self.nin = 0

    def __init__(self, nc):
        self.nc = nc
        self.streams = {}
        self.by_eng = {e: [] for e in ("tensor", "vector", "scalar",
                                       "gpsimd", "sync")}
        self.sem = {}
        self.last_w = {}
        self.readers = {}
        self.pending = {e: set() for e in self.by_eng}
        self.anchor = {}
        self.prev = None
        self.order = []

    def emit(self, eng, fn, r=(), w=(), dma=False, cost=None, lat=None,
             chan=None):
        if dma:
            assert eng in ("gpsimd", "sync")
            q = "gdma" if eng == "gpsimd" else "sdma"
            stream = q + ":" + (chan if chan is not None else (w[0] if w else "_"))
        else:
            stream = eng
        if stream not in self.streams:
            self.streams[stream] = []
            self.sem[stream] = self.nc.alloc_semaphore(
                "s_" + stream.replace(":", "_"))
        idx = len(self.streams[stream])
        deps = set()
        for k in r:
            lw = self.last_w.get(k)
            if lw is not None:
                deps.add(lw)
        for k in w:
            lw = self.last_w.get(k)
            if lw is not None:
                deps.add(lw)
            deps |= self.readers.get(k, set())
        eng_real = self._eng_of(stream)
        fence = False
        if self.pending[eng_real]:
            deps |= self.pending[eng_real]
            self.pending[eng_real] = set()
            fence = True
        elif self.anchor.get(eng_real) is not None:
            deps.add(self.anchor[eng_real])
        if _SERIAL and self.prev is not None:
            deps.add(self.prev)
        if cost is None:
            cost = self.DMA_ISSUE if dma else self.DEF_COST[eng_real]
        if lat is None:
            lat = self.DMA_LAT if dma else 0.0
        op = self.Op(eng_real, stream, idx, fn, dma, deps, cost, lat,
                     len(self.order))
        if fence:
            self.anchor[eng_real] = op
        self.prev = op
        self.streams[stream].append(op)
        self.by_eng[eng_real].append(op)
        self.order.append(op)
        for k in r:
            self.readers.setdefault(k, set()).add(op)
        for k in w:
            self.last_w[k] = op
            self.readers[k] = set()
        return op

    def barrier(self):
        snap = set(self.order)
        for e in self.pending:
            self.pending[e] = set(snap)
        self.anchor = {}

    def _reorder(self):
        import heapq
        ops = self.order
        for op in ops:
            op.succs = []
        for op in ops:
            op.nin = len(op.deps)
            for d in op.deps:
                d.succs.append(op)
        for op in reversed(ops):
            m = 0.0
            for s in op.succs:
                if s.prio > m:
                    m = s.prio
            op.prio = op.cost + op.lat + m
        engines = list(self.by_eng.keys())
        avail_h = {e: [] for e in engines}
        ready_h = {e: [] for e in engines}
        wake = []
        free = {e: 0.0 for e in engines}

        def release(op):
            heapq.heappush(avail_h[op.eng], (op.avail, op.seq, op))
            heapq.heappush(wake, (op.avail, op.seq, op.eng))

        for op in ops:
            if op.nin == 0:
                release(op)
        new_order = []
        n = len(ops)
        while len(new_order) < n:
            t, _, e = heapq.heappop(wake)
            t = max(t, free[e])
            ah, rh = avail_h[e], ready_h[e]
            while ah and ah[0][0] <= t:
                _, sq, op = heapq.heappop(ah)
                heapq.heappush(rh, (-op.prio, sq, op))
            if not rh:
                if ah:
                    heapq.heappush(wake, (max(ah[0][0], free[e]), ah[0][1], e))
                continue
            _, _, op = heapq.heappop(rh)
            start = t
            busy = self.DMA_ISSUE if op.dma else op.cost
            free[e] = start + busy
            data_t = start + op.cost + op.lat
            new_order.append(op)
            for s in op.succs:
                dt_s = data_t if s.eng == op.eng and not op.dma else data_t + 250.0
                if dt_s > s.avail:
                    s.avail = dt_s
                s.nin -= 1
                if s.nin == 0:
                    release(s)
            if ah or rh:
                nt = free[e]
                if not rh and ah and ah[0][0] > nt:
                    nt = ah[0][0]
                heapq.heappush(wake, (nt, op.seq, e))
        self.order = new_order
        for st in self.streams:
            self.streams[st] = []
        for e in self.by_eng:
            self.by_eng[e] = []
        for op in new_order:
            op.idx = len(self.streams[op.stream])
            self.streams[op.stream].append(op)
            self.by_eng[op.eng].append(op)

    def _keep_edge(self, dep_op, op):
        if _NO_SKIP:
            return True
        if dep_op.stream == op.stream:
            if dep_op.dma and op.dma:
                return False
            if op.stream == "tensor":
                return False
        if (not _SAME_ENGINE_EDGES and dep_op.eng == op.eng
                and not dep_op.dma and dep_op.stream != op.stream):
            return False
        return True

    def finalize(self):
        if _RESCHED and not _SERIAL:
            self._reorder()
        for ops in self.by_eng.values():
            for op in ops:
                best = {}
                for dep_op in op.deps:
                    if (self._keep_edge(dep_op, op)
                            and dep_op.idx > best.get(dep_op.stream, (-1, None))[0]):
                        best[dep_op.stream] = (dep_op.idx, dep_op)
                kept = [v[1] for v in best.values()]
                for dep_op in kept:
                    dep_op.inc = True
                op.deps = kept
        if _SERIAL:
            for ops in self.streams.values():
                for op in ops:
                    op.inc = True
        for ops in self.streams.values():
            c = 0
            for op in ops:
                if op.inc:
                    c += 16 if op.dma else 1
                op.cum = c
        for eng, ops in self.by_eng.items():
            hwm = {}
            for op in ops:
                need = {}
                for dep_op in op.deps:
                    st = dep_op.stream
                    v = dep_op.cum
                    if v > need.get(st, -1):
                        need[st] = v
                for st, v in sorted(need.items()):
                    if v > hwm.get(st, 0):
                        op.waits.append((self.sem[st], v))
                        hwm[st] = v

    def totals(self):
        out = {}
        for st, ops in self.streams.items():
            out[st] = ops[-1].cum if ops else 0
        return out

    def replay(self, block):
        nc = self.nc
        self.finalize()
        tot = self.totals()
        engobj = {"tensor": nc.tensor, "vector": nc.vector,
                  "scalar": nc.scalar, "gpsimd": nc.gpsimd, "sync": nc.sync}

        def body(eng):
            e = engobj[eng]
            for op in self.by_eng[eng]:
                for (sem, v) in op.waits:
                    e.wait_ge(sem, v)
                inst = op.fn(e)
                if op.inc:
                    inst.then_inc(self.sem[op.stream], 16 if op.dma else 1)
            if eng == "sync":
                for st in sorted(tot):
                    if tot[st]:
                        e.wait_ge(self.sem[st], tot[st])

        block.tensor(lambda _: body("tensor"))
        block.vector(lambda _: body("vector"))
        block.scalar(lambda _: body("scalar"))
        block.gpsimd(lambda _: body("gpsimd"))
        block.sync(lambda _: body("sync"))


def build_program():
    NCA = T // TCA
    TCB = min(1024, T)
    NCB = T // TCB
    TTB = TCB // P

    nc = bass.Bass("TRN2", target_bir_lowering=False)
    dt = nc.dram_tensor

    x_d = dt("x", [T, E], F32, kind="ExternalInput").ap()
    wi_d = dt("wi_shuf", [NO, P, NEc * P], F32, kind="ExternalInput").ap()
    bin_d = dt("b_in_t", [P, NO], F32, kind="ExternalInput").ap()
    cw_d = dt("conv_w_t", [P, NO * 4], F32, kind="ExternalInput").ap()
    cb_d = dt("conv_b_t", [P, NO], F32, kind="ExternalInput").ap()
    wo_d = dt("wo_shuf", [4, P, 16 * 512], F32, kind="ExternalInput").ap()
    bout_d = dt("bout_bc", [P, E], F32, kind="ExternalInput").ap()
    wg_d = dt("wg_shuf", [P, NEc * 8], F32, kind="ExternalInput").ap()
    wgu_d = dt("wgu_shuf", [NEXP, 4, 2, P, 16 * 256], F32, kind="ExternalInput").ap()
    wd_d = dt("wd_shuf", [NEXP, 4, P, 8 * 512], F32, kind="ExternalInput").ap()
    ident_d = dt("c_ident", [P, P], BF16, kind="ExternalInput").ap()
    tri_d = dt("c_tri", [P, P], BF16, kind="ExternalInput").ap()
    swap_d = dt("c_swap", [P, P], F32, kind="ExternalInput").ap()
    cos_d = dt("c_cos", [P, T], F32, kind="ExternalInput").ap()
    sin_d = dt("c_sin", [P, T], F32, kind="ExternalInput").ap()

    y_d = dt("y_scr", [T, E], F32,
             kind="ExternalOutput" if _DBG else "Internal").ap()
    out_d = dt("out", [T, E], F32, kind="ExternalOutput").ap()

    s = Sched(nc)
    emit = s.emit

    top = contextlib.ExitStack()

    def SBt(st, name, shape, dtyp):
        return st.enter_context(nc.sbuf_tensor(name, shape, dtyp))

    def PSt(st, name, shape, dtyp=F32):
        return st.enter_context(nc.psum_tensor(name, shape, dtyp))

    ident_s = SBt(top, "ident_s", [P, P], BF16)
    emit("sync", lambda e: e.dma_start(out=ident_s[:], in_=ident_d),
         w=["ident"], dma=True)

    stA = contextlib.ExitStack()
    Ksb = SBt(stA, "Ksb", [P, HKV * T], BF16)
    Vsb = SBt(stA, "Vsb", [P, (T // P) * 512], BF16)
    wi_sl = [SBt(stA, f"wi_sl{i}", [P, NEc * P], BF16) for i in range(2)]
    wo_sl = [SBt(stA, f"wo_sl{i}", [P, 8 * 512], BF16) for i in range(2)]
    x_bf = SBt(stA, "x_bf", [P, E], BF16)
    x_fm = SBt(stA, "x_fm", [P, NEc * TCA], BF16)
    x_f32 = SBt(stA, "x_f32", [P, 4 * E], F32)
    Qsb = SBt(stA, "Qsb", [P, H * TCA], BF16)
    ctxfm = SBt(stA, "ctxfm", [P, H * TCA], BF16)
    qkv_o = [SBt(stA, f"qkv_o{i}", [P, 516], F32) for i in range(2)]
    conv_o = [SBt(stA, f"conv_o{i}", [P, TCA], F32) for i in range(2)]
    rot_t = [SBt(stA, "rot_t0", [P, TCA], F32)] * 2
    rot2_t = [SBt(stA, "rot2_t0", [P, TCA], F32)] * 2
    swap_s = SBt(stA, "swap_s", [P, P], F32)
    cos_s = SBt(stA, "cos_s", [P, TCA], F32)
    sin_s = SBt(stA, "sin_s", [P, TCA], F32)
    probs_bf = [SBt(stA, f"probs_bf{i}", [P, 1024], BF16) for i in range(2)]
    pT_sb = [SBt(stA, f"pT_sb{i}", [P, 1024], BF16) for i in range(2)]
    acc_fl = [SBt(stA, f"acc_fl{i}", [P, P], F32) for i in range(10)]
    ctx_bt = [SBt(stA, f"ctx_bt{i}", [P, P], BF16) for i in range(10)]
    v_bf = SBt(stA, "v_bf", [P, TCA], BF16)
    bin_s = SBt(stA, "bin_s", [P, NO], F32)
    cw_s = SBt(stA, "cw_s", [P, NO * 4], F32)
    cb_s = SBt(stA, "cb_s", [P, NO], F32)
    tri_s = SBt(stA, "tri_s", [P, P], BF16)
    B3 = SBt(stA, "B3", [P, NO * 4], F32)
    smA = SBt(stA, "smA", [P, 124], F32)
    ps_all = PSt(stA, "ps_all", [P, 4 * 512])
    ps_pTl = [PSt(stA, f"ps_pT{i}", [P, 1024], BF16) for i in range(2)]
    ps_pT = ps_pTl[0]
    ps_ctxl = [PSt(stA, f"ps_ctx{i}", [P, 512]) for i in range(2)]

    def psf(i):
        return ps_all[:, i * 512:(i + 1) * 512], f"psc{i}"

    MSUM, VSUM, LNT, EPSV = range(4)
    LRUN, LINV, CORR0, CORR1 = range(4)
    NMH0, SSH0 = 4, 8
    LSC0 = NMH0

    def sv(i):
        return smA[:, i:i + 1]

    def svk(i):
        return f"smA{i}"

    def fv(sl, i):
        return smA[:, 4 + sl * 12 + i: 5 + sl * 12 + i]

    def fvr(sl, i, n):
        return smA[:, 4 + sl * 12 + i: 4 + sl * 12 + i + n]

    def fvk(sl, i):
        return f"smF{sl}_{i}"

    emit("sync", lambda e: e.dma_start(out=bin_s[:], in_=bin_d), w=["bin"], dma=True)
    emit("sync", lambda e: e.dma_start(out=cw_s[:], in_=cw_d), w=["cw"], dma=True)
    emit("sync", lambda e: e.dma_start(out=cb_s[:], in_=cb_d), w=["cb"], dma=True)
    emit("sync", lambda e: e.dma_start(out=tri_s[:], in_=tri_d), w=["tri"], dma=True)
    emit("sync", lambda e: e.dma_start(out=swap_s[:], in_=swap_d), w=["swap"], dma=True)
    emit("gpsimd", lambda e: e.memset(B3[:], 0.0), w=["B3"])
    emit("gpsimd", lambda e: e.memset(sv(EPSV), EPS), w=[svk(EPSV)])

    for c in range(NCA):
        t0 = c * TCA
        for r in range(4):
            src = x_d[t0 + r * P: t0 + (r + 1) * P, :]
            emit("gpsimd", lambda e, src=src: e.dma_start(
                out=x_bf[:, :], in_=src),
                w=["xbf"], dma=True, lat=7000, cost=350)
            for eh in range(NEc // 2):
                for tl in range(2):
                    ec = eh * 2 + tl
                    emit("tensor", lambda e, ec=ec, tl=tl: e.transpose(
                        ps_pT[:, tl * P:(tl + 1) * P],
                        x_bf[:, ec * P:(ec + 1) * P],
                        ident_s[:]),
                        r=["xbf", "ident"], w=["pspT0"], cost=180)
                emit("scalar", lambda e, eh=eh, r=r: e.copy(
                    x_fm[:, :].rearrange("p (a b) -> p a b", a=NEc)[
                        :, 2 * eh:2 * eh + 2, r * P:(r + 1) * P],
                    ps_pT[:, 0:256].rearrange("p (a b) -> p a b", a=2)),
                    r=["pspT0"], w=["xfm"])
        emit("sync", lambda e, c=c: e.dma_start(
            out=cos_s[:], in_=cos_d[:, c * TCA:(c + 1) * TCA]),
            r=["cosuse"], w=["cos"], dma=True)
        emit("sync", lambda e, c=c: e.dma_start(
            out=sin_s[:], in_=sin_d[:, c * TCA:(c + 1) * TCA]),
            r=["sinuse"], w=["sin"], dma=True)

        for o in range(NO):
            pb = o % 2
            emit("gpsimd", lambda e, o=o, pb=pb: e.dma_start(
                out=wi_sl[pb][:], in_=wi_d[o]), w=[f"wi{pb}"], dma=True,
                lat=9000, cost=350)
            for a in range(NEc):
                emit("tensor", lambda e, a=a, pb=pb: e.matmul(
                    psf(2 * pb)[0], wi_sl[pb][:, a * P:(a + 1) * P],
                    x_fm[:, a * TCA:(a + 1) * TCA],
                    start=(a == 0), stop=(a == NEc - 1)),
                    r=[f"wi{pb}", "xfm"], w=[f"psc{2 * pb}"], cost=360)
            emit("scalar", lambda e, o=o, pb=pb: e.activation(
                qkv_o[pb][:, 3:515], psf(2 * pb)[0], AF.Identity,
                bias=bin_s[:, o:o + 1], scale=1.0),
                r=[f"psc{2 * pb}", "bin"], w=[f"qkv{pb}"], cost=830)
            emit("vector", lambda e, o=o, pb=pb: e.tensor_copy(
                qkv_o[pb][:, 0:3], B3[:, o * 4: o * 4 + 3]),
                r=["B3"], w=[f"qkv{pb}"])
            emit("vector", lambda e, o=o, pb=pb: e.tensor_copy(
                B3[:, o * 4: o * 4 + 3], qkv_o[pb][:, 512:515]),
                r=[f"qkv{pb}"], w=["B3"])
            ceng = "vector"
            emit(ceng, lambda e, o=o, pb=pb: e.tensor_scalar(
                conv_o[pb][:], qkv_o[pb][:, 0:512],
                cw_s[:, o * 4: o * 4 + 1], cb_s[:, o:o + 1],
                ALU.mult, ALU.add),
                r=[f"qkv{pb}", "cw", "cb"], w=[f"conv{pb}"], cost=900)
            for k in range(1, 4):
                emit(ceng, lambda e, o=o, pb=pb, k=k: e.scalar_tensor_tensor(
                    conv_o[pb][:], qkv_o[pb][:, k:k + 512],
                    cw_s[:, o * 4 + k: o * 4 + k + 1],
                    conv_o[pb][:], ALU.mult, ALU.add),
                    r=[f"qkv{pb}", "cw", f"conv{pb}"], w=[f"conv{pb}"], cost=900)
            if o < 20:
                emit("tensor", lambda e, pb=pb: e.matmul(
                    psf(1)[0], swap_s[:], conv_o[pb][:],
                    start=True, stop=True),
                    r=["swap", f"conv{pb}"], w=["psc1"], cost=1150)
                emit("gpsimd", lambda e, pb=pb: e.tensor_mul(
                    rot_t[pb][:], conv_o[pb][:], cos_s[:]),
                    r=[f"conv{pb}", "cos", "cosuse"], w=["rott"], cost=900)
                emit("vector", lambda e, pb=pb: e.tensor_mul(
                    rot2_t[pb][:], psf(1)[0], sin_s[:]),
                    r=["psc1", "sin", "sinuse"], w=["rot2t"], cost=830)
                if o < 16:
                    dst = Qsb[:, o * TCA:(o + 1) * TCA]
                    dkey = "Q"
                else:
                    dst = Ksb[:, (o - 16) * T + t0: (o - 16) * T + t0 + TCA]
                    dkey = f"Kc{c}"
                emit("vector", lambda e, dst=dst, pb=pb: e.tensor_add(
                    dst, rot_t[pb][:], rot2_t[pb][:]),
                    r=["rott", "rot2t"], w=[dkey], cost=560)
            else:
                kv = o - 20
                emit("vector", lambda e, pb=pb: e.tensor_copy(
                    v_bf[:], conv_o[pb][:]),
                    r=[f"conv{pb}"], w=["vbf"], cost=560)
                for j in range(4):
                    emit("tensor", lambda e, j=j: e.transpose(
                        ps_pT[:, j * P:(j + 1) * P],
                        v_bf[:, j * P:(j + 1) * P], ident_s[:]),
                        r=["vbf", "ident"], w=["pspT0"], cost=180)
                for j in range(4):
                    kt = 4 * c + j
                    emit("scalar", lambda e, kt=kt, kv=kv, j=j: e.copy(
                        Vsb[:, kt * 512 + kv * P: kt * 512 + (kv + 1) * P],
                        ps_pT[:, j * P:(j + 1) * P]),
                        r=["pspT0"], w=[f"Vc{c}"])

        GK = 8

        def scfull(lane, span):
            return ps_all[:, lane * 1024: lane * 1024 + span]

        def sckeys(lane, span):
            ks = [f"psc{lane * 2}"]
            if span > 512:
                ks.append(f"psc{lane * 2 + 1}")
            return ks

        def fl_span(gi, g):
            return min(GK, g + 1 - GK * gi) * P

        def fl_scores(lane, h, qt, gi, g):
            kv = h // 4
            kb0 = GK * gi
            nb = min(GK, g + 1 - kb0)
            span = nb * P
            diag = (kb0 + nb - 1 == g)
            kck = [f"Kc{i}" for i in range(kb0 // 4, (kb0 + nb - 1) // 4 + 1)]
            if span > 512:
                parts = [(0, 512, False), (512, span, diag)]
            else:
                parts = [(0, span, diag)]
            for (c0, c1, dg) in parts:
                key = f"psc{lane * 2 + c0 // 512}"
                emit("tensor",
                     lambda e, h=h, qt=qt, kv=kv, kb0=kb0, c0=c0, c1=c1,
                     lane=lane, dg=dg:
                     e.matmul(ps_all[:, lane * 1024 + c0: lane * 1024 + c1],
                              Qsb[:, h * TCA + qt * P: h * TCA + (qt + 1) * P],
                              Ksb[:, kv * T + kb0 * P + c0: kv * T + kb0 * P + c1],
                              start=True, stop=not dg),
                     r=["Q"] + kck, w=[key], cost=(c1 - c0) * 0.7 + 120)
                if dg:
                    emit("tensor", lambda e, lane=lane, c1=c1: e.matmul(
                        ps_all[:, lane * 1024 + c1 - P: lane * 1024 + c1],
                        ident_s[:], tri_s[:],
                        start=False, stop=True),
                        r=["ident", "tri"], w=[key], cost=180)

        def fl_gmax(lane, sl, gi, g):
            span = fl_span(gi, g)
            emit("vector", lambda e, span=span, sl=sl, lane=lane: e.reduce_max(
                fv(sl, LINV), scfull(lane, span), axis=AX),
                r=sckeys(lane, span), w=[fvk(sl, LINV)], cost=span * 1.35 + 250)

        def fl_nm(sl, gi):
            if gi == 0:
                emit("vector", lambda e, sl=sl: e.tensor_scalar(
                    fv(sl, NMH0), fv(sl, LINV), -SCALE, None, ALU.mult),
                    r=[fvk(sl, LINV)], w=[fvk(sl, NMH0)], cost=220)
            else:
                emit("vector", lambda e, sl=sl, gi=gi: e.tensor_scalar(
                    fv(sl, NMH0 + gi), fv(sl, LINV), -SCALE,
                    fv(sl, NMH0 + gi - 1), ALU.mult, ALU.min),
                    r=[fvk(sl, LINV), fvk(sl, NMH0 + gi - 1)],
                    w=[fvk(sl, NMH0 + gi)], cost=220)

        def fl_corr(sl, gi):
            cur = gi % 2
            emit("scalar", lambda e, sl=sl, gi=gi, cur=cur: e.activation(
                fv(sl, CORR0 + cur), fv(sl, NMH0 + gi - 1), AF.Exp,
                bias=fv(sl, NMH0 + gi), scale=-1.0),
                r=[fvk(sl, NMH0 + gi - 1), fvk(sl, NMH0 + gi)],
                w=[fvk(sl, CORR0 + cur)], cost=240)

        def fl_exp(lane, sl, gi, g):
            span = fl_span(gi, g)
            emit("scalar", lambda e, span=span, lane=lane, sl=sl, gi=gi:
                 e.activation(
                     probs_bf[lane][:, 0:span], scfull(lane, span), AF.Exp,
                     bias=fv(sl, NMH0 + gi), scale=SCALE,
                     accum_out=fv(sl, SSH0 + gi)),
                 r=sckeys(lane, span) + [fvk(sl, NMH0 + gi)],
                 w=[f"probs{lane}", fvk(sl, SSH0 + gi)],
                 cost=span * 1.15 + 220)

        def fl_transp(lane, gi, g):
            span = fl_span(gi, g)
            for j in range(span // P):
                emit("tensor", lambda e, j=j, lane=lane: e.transpose(
                    ps_pTl[lane][:, j * P:(j + 1) * P],
                    probs_bf[lane][:, j * P:(j + 1) * P], ident_s[:]),
                    r=[f"probs{lane}", "ident"], w=[f"pspT{lane}"], cost=180)

        def fl_evac(lane, gi, g):
            span = fl_span(gi, g)
            h1 = min(span, 512)
            emit("scalar", lambda e, h1=h1, lane=lane: e.copy(
                pT_sb[lane][:, 0:h1], ps_pTl[lane][:, 0:h1]),
                r=[f"pspT{lane}"], w=[f"pTsb{lane}"], cost=h1 * 1.1 + 200)
            if span > 512:
                emit("vector", lambda e, span=span, lane=lane: e.tensor_copy(
                    pT_sb[lane][:, 512:span], ps_pTl[lane][:, 512:span]),
                    r=[f"pspT{lane}"], w=[f"pTsb{lane}"],
                    cost=(span - 512) * 0.6 + 220)

        def fl_ctx(lane, h, qt, gi, g):
            kv = h // 4
            kb0 = GK * gi
            nb = fl_span(gi, g) // P
            vck = [f"Vc{i}" for i in range(kb0 // 4, (kb0 + nb - 1) // 4 + 1)]
            for j in range(nb):
                kt = kb0 + j
                emit("tensor", lambda e, j=j, kt=kt, kv=kv, nb=nb, lane=lane:
                     e.matmul(ps_ctxl[lane][:, 0:P],
                              pT_sb[lane][:, j * P:(j + 1) * P],
                              Vsb[:, kt * 512 + kv * P: kt * 512 + (kv + 1) * P],
                              start=(j == 0), stop=(j == nb - 1)),
                     r=[f"pTsb{lane}"] + vck, w=[f"psctx{lane}"], cost=180)

        def fl_acc(lane, sl, gi):
            if gi == 0:
                emit("vector", lambda e, lane=lane, sl=sl: e.tensor_copy(
                    acc_fl[sl][:], ps_ctxl[lane][:, 0:P]),
                    r=[f"psctx{lane}"], w=[f"accfl{sl}"], cost=330)
            else:
                cur = gi % 2
                emit("vector", lambda e, lane=lane, sl=sl, cur=cur:
                     e.scalar_tensor_tensor(
                         acc_fl[sl][:], acc_fl[sl][:], fv(sl, CORR0 + cur),
                         ps_ctxl[lane][:, 0:P], ALU.mult, ALU.add),
                     r=[f"accfl{sl}", fvk(sl, CORR0 + cur), f"psctx{lane}"],
                     w=[f"accfl{sl}"], cost=420)

        def flash_final(h, qt, lane, sl, ng):
            nmk = [fvk(sl, NMH0 + i) for i in range(ng)]
            ssk = [fvk(sl, SSH0 + i) for i in range(ng)]
            lsck = f"lsc{sl}"
            emit("scalar", lambda e, sl=sl, ng=ng: e.activation(
                fvr(sl, LSC0, ng), fvr(sl, NMH0, ng), AF.Exp,
                bias=fv(sl, NMH0 + ng - 1), scale=-1.0),
                r=nmk, w=[lsck] + nmk, cost=240)
            emit("vector", lambda e, sl=sl, ng=ng: e.tensor_mul(
                fvr(sl, LSC0, ng), fvr(sl, LSC0, ng), fvr(sl, SSH0, ng)),
                r=[lsck] + ssk, w=[lsck], cost=220)
            emit("vector", lambda e, sl=sl, ng=ng: e.reduce_sum(
                fv(sl, LRUN), fvr(sl, LSC0, ng), axis=AX),
                r=[lsck], w=[fvk(sl, LRUN)], cost=220)
            emit("vector", lambda e, sl=sl: e.reciprocal(
                fv(sl, LINV), fv(sl, LRUN)),
                r=[fvk(sl, LRUN)], w=[fvk(sl, LINV)], cost=220)
            emit("vector", lambda e, sl=sl: e.tensor_scalar(
                ctx_bt[sl][:], acc_fl[sl][:], fv(sl, LINV), None, ALU.mult),
                r=[f"accfl{sl}", fvk(sl, LINV)], w=[f"ctxbt{sl}"], cost=360)
            emit("tensor", lambda e, lane=lane, sl=sl: e.transpose(
                ps_pTl[lane][:, 0:P], ctx_bt[sl][:], ident_s[:]),
                r=[f"ctxbt{sl}", "ident"], w=[f"pspT{lane}"], cost=180)
            emit("scalar", lambda e, h=h, qt=qt, lane=lane: e.copy(
                ctxfm[:, h * TCA + qt * P: h * TCA + (qt + 1) * P],
                ps_pTl[lane][:, 0:P]),
                r=[f"pspT{lane}"], w=["ctxfm"], cost=440)

        iters = [(h, qt) for h in range(H) for qt in range(4)]
        for i0 in range(0, len(iters), 2):
            pair = iters[i0:i0 + 2]
            pp = (i0 // 2) % 5
            gs = [4 * c + qt for (_, qt) in pair]
            ngs = [(g + GK) // GK for g in gs]
            for gi in range(max(ngs) + 1):
                curw = [(lane, h, qt) for lane, (h, qt) in enumerate(pair)
                        if gi < ngs[lane]]
                tailw = [(lane, h, qt) for lane, (h, qt) in enumerate(pair)
                         if 0 <= gi - 1 < ngs[lane]]
                for (lane, h, qt) in curw:
                    fl_scores(lane, h, qt, gi, gs[lane])
                for (lane, h, qt) in tailw:
                    fl_transp(lane, gi - 1, gs[lane])
                for (lane, h, qt) in tailw:
                    fl_evac(lane, gi - 1, gs[lane])
                for (lane, h, qt) in tailw:
                    fl_ctx(lane, h, qt, gi - 1, gs[lane])
                for (lane, h, qt) in curw:
                    fl_gmax(lane, lane + 2 * pp, gi, gs[lane])
                for (lane, h, qt) in curw:
                    fl_nm(lane + 2 * pp, gi)
                for (lane, h, qt) in tailw:
                    fl_acc(lane, lane + 2 * pp, gi - 1)
                for (lane, h, qt) in curw:
                    if gi >= 1:
                        fl_corr(lane + 2 * pp, gi)
                for (lane, h, qt) in curw:
                    fl_exp(lane, lane + 2 * pp, gi, gs[lane])
            for lane, (h, qt) in enumerate(pair):
                flash_final(h, qt, lane, lane + 2 * pp, ngs[lane])

        for tt in range(4):
            emit("sync", lambda e, tt=tt, t0=t0: e.dma_start(
                out=x_f32[:, tt * E:(tt + 1) * E],
                in_=x_d[t0 + tt * P: t0 + (tt + 1) * P, :]),
                w=[f"xf32t{tt}"], dma=True, chan=f"xf32t{tt}", lat=2600)
        for tt in range(4):
            emit("gpsimd", lambda e, tt=tt: e.dma_start(
                out=x_f32[:, tt * E:(tt + 1) * E], in_=bout_d,
                accum_op=ALU.add),
                r=[f"xf32t{tt}"], w=[f"xf32t{tt}"], dma=True,
                chan=f"xfa{tt}")
        for og in range(4):
            for fh in range(2):
                emit("gpsimd", lambda e, og=og, fh=fh: e.dma_start(
                    out=wo_sl[fh][:],
                    in_=wo_d[og, :, fh * 4096:(fh + 1) * 4096]),
                    w=[f"wo{fh}"], dma=True, lat=14000, cost=400)
                for f8 in range(8):
                    f = fh * 8 + f8
                    for tt in range(4):
                        emit("tensor",
                             lambda e, f=f, f8=f8, fh=fh, tt=tt:
                             e.matmul(psf(tt)[0],
                                      ctxfm[:, f * TCA + tt * P: f * TCA + (tt + 1) * P],
                                      wo_sl[fh][:, f8 * 512:(f8 + 1) * 512],
                                      start=(f == 0), stop=(f == 15)),
                             r=["ctxfm", f"wo{fh}"], w=[f"psc{tt}"], cost=360)
            for tt in range(4):
                emit("vector", lambda e, tt=tt, og=og: e.tensor_add(
                    x_f32[:, tt * E + og * 512: tt * E + (og + 1) * 512],
                    x_f32[:, tt * E + og * 512: tt * E + (og + 1) * 512],
                    psf(tt)[0]),
                    r=[f"xf32t{tt}", f"psc{tt}"], w=[f"xf32t{tt}"], cost=830)
        for tt in range(4):
            sl = x_f32[:, tt * E:(tt + 1) * E]
            emit("vector", lambda e, sl=sl: e.reduce_sum(sv(MSUM), sl, axis=AX),
                 r=[f"xf32t{tt}"], w=[svk(MSUM)], cost=2400)
            emit("vector", lambda e: e.tensor_scalar(
                sv(MSUM), sv(MSUM), 1.0 / E, None, ALU.mult),
                r=[svk(MSUM)], w=[svk(MSUM)])
            emit("vector", lambda e, sl=sl: e.tensor_scalar(
                sl, sl, sv(MSUM), None, ALU.subtract),
                r=[f"xf32t{tt}", svk(MSUM)], w=[f"xf32t{tt}"], cost=2400)
            emit("scalar", lambda e, sl=sl: e.activation(
                x_bf[:, 0:E], sl, AF.Square, accum_out=sv(VSUM)),
                r=[f"xf32t{tt}"], w=["xbf", svk(VSUM)], cost=2600)
            emit("vector", lambda e: e.tensor_scalar(
                sv(VSUM), sv(VSUM), 1.0 / E, None, ALU.mult),
                r=[svk(VSUM)], w=[svk(VSUM)])
            emit("scalar", lambda e: e.activation(
                sv(LNT), sv(VSUM), AF.Ln, bias=sv(EPSV), scale=1.0),
                r=[svk(VSUM), svk(EPSV)], w=[svk(LNT)])
            emit("scalar", lambda e: e.activation(
                sv(LNT), sv(LNT), AF.Exp, scale=-0.5),
                r=[svk(LNT)], w=[svk(LNT)])
            emit("vector", lambda e, sl=sl: e.tensor_scalar(
                sl, sl, sv(LNT), None, ALU.mult),
                r=[f"xf32t{tt}", svk(LNT)], w=[f"xf32t{tt}"], cost=2400)
            emit("sync", lambda e, sl=sl, tt=tt, t0=t0: e.dma_start(
                out=y_d[t0 + tt * P: t0 + (tt + 1) * P, :], in_=sl),
                r=[f"xf32t{tt}"], w=[f"yc{c}"], dma=True, chan=f"y{tt}")

    stA.close()
    s.barrier()

    stB = contextlib.ExitStack()
    accB = SBt(stB, "accB", [P, TTB * E], F32)
    y_fm = SBt(stB, "y_fm", [P, NEc * TCB], BF16)
    act_fm = SBt(stB, "act_fm", [P, 2 * TTB * 1024], BF16)
    wgu_g = [SBt(stB, f"wgu_g{i}", [P, 16 * 256], BF16) for i in range(2)]
    wgu_u = [SBt(stB, f"wgu_u{i}", [P, 16 * 256], BF16) for i in range(2)]
    wd_sl = [SBt(stB, f"wd_sl{i}", [P, 8 * 512], BF16) for i in range(2)]
    y_tmb = [SBt(stB, f"y_tmb{i}", [P, E], BF16) for i in range(2)]
    y_f32 = [SBt(stB, f"y_f32{i}", [P, E], F32) for i in range(2)]
    silu_t = SBt(stB, "silu_t", [P, 256], F32)
    act_bt = SBt(stB, "act_bt", [P, 256], BF16)
    wg_s = SBt(stB, "wg_s", [P, NEc * 8], BF16)
    lg_s = SBt(stB, "lg_s", [P, 8], F32)
    p8_s = SBt(stB, "p8_s", [P, 8], F32)
    t8_s = SBt(stB, "t8_s", [P, 8], F32)
    comb = SBt(stB, "comb", [P, TTB * 8], F32)
    smB = SBt(stB, "smB", [P, 16], F32)
    ps_gg = [PSt(stB, f"ps_gg{i}", [P, 512]) for i in range(2)]
    ps_gu = [PSt(stB, f"ps_gu{i}", [P, 512]) for i in range(2)]
    ps_atr = PSt(stB, "ps_atr", [P, 1024], BF16)
    ps_d = [PSt(stB, f"ps_d{i}", [P, 512]) for i in range(2)]
    ps_ytr = PSt(stB, "ps_ytr", [P, E // 2], BF16)

    R1, NR1, R2, RS, RSI, BMS, BVS, BLT, BEPS = range(9)

    def sb(i):
        return smB[:, i:i + 1]

    def sbk(i):
        return f"smB{i}"

    emit("gpsimd", lambda e: e.dma_start(out=wg_s[:], in_=wg_d),
         w=["wg"], dma=True)
    emit("gpsimd", lambda e: e.memset(sb(BEPS), EPS), w=[sbk(BEPS)])

    for cb in range(NCB):
        t0 = cb * TCB
        for tt in range(TTB):
            pb = tt % 2
            yck = f"yc{(t0 + tt * P) // TCA}"
            emit("gpsimd", lambda e, tt=tt, pb=pb, t0=t0: e.dma_start(
                out=y_tmb[pb][:],
                in_=y_d[t0 + tt * P: t0 + (tt + 1) * P, :]),
                r=[yck], w=[f"ytmb{pb}"], dma=True, lat=2200)
            for half in range(2):
                for eh in range(NEc // 2):
                    ec = half * (NEc // 2) + eh
                    emit("tensor", lambda e, ec=ec, eh=eh, pb=pb: e.transpose(
                        ps_ytr[:, eh * P:(eh + 1) * P],
                        y_tmb[pb][:, ec * P:(ec + 1) * P], ident_s[:]),
                        r=[f"ytmb{pb}", "ident"], w=["psytr"], cost=180)
                emit("vector", lambda e, tt=tt, half=half: e.tensor_copy(
                    y_fm[:, :].rearrange("p (a b) -> p a b", a=NEc)[
                        :, half * (NEc // 2):(half + 1) * (NEc // 2),
                        tt * P:(tt + 1) * P],
                    ps_ytr[:, :].rearrange("p (a b) -> p a b", a=NEc // 2)),
                    r=["psytr"], w=["yfm"], cost=700)
        for tt in range(TTB):
            for a in range(NEc):
                emit("tensor", lambda e, a=a, tt=tt: e.matmul(
                    ps_d[0][:, 0:8], y_fm[:, a * TCB + tt * P: a * TCB + (tt + 1) * P],
                    wg_s[:, a * 8:(a + 1) * 8],
                    start=(a == 0), stop=(a == NEc - 1)),
                    r=["yfm", "wg"], w=["psd0"], cost=65)
            emit("vector", lambda e: e.tensor_copy(lg_s[:], ps_d[0][:, 0:8]),
                 r=["psd0"], w=["lg"])
            emit("vector", lambda e: e.reduce_max(sb(R1), lg_s[:], axis=AX),
                 r=["lg"], w=[sbk(R1)])
            emit("vector", lambda e: e.tensor_scalar(
                sb(NR1), sb(R1), -1.0, None, ALU.mult),
                r=[sbk(R1)], w=[sbk(NR1)])
            emit("vector", lambda e: e.tensor_scalar(
                t8_s[:], lg_s[:], sb(R1), None, ALU.is_ge),
                r=["lg", sbk(R1)], w=["t8"])
            emit("vector", lambda e: e.scalar_tensor_tensor(
                t8_s[:], t8_s[:], NEG, lg_s[:], ALU.mult, ALU.add),
                r=["t8", "lg"], w=["t8"])
            emit("vector", lambda e: e.reduce_max(sb(R2), t8_s[:], axis=AX),
                 r=["t8"], w=[sbk(R2)])
            emit("vector", lambda e: e.tensor_scalar(
                t8_s[:], lg_s[:], sb(R2), None, ALU.is_ge),
                r=["lg", sbk(R2)], w=["t8"])
            emit("scalar", lambda e: e.activation(
                p8_s[:], lg_s[:], AF.Exp, bias=sb(NR1), scale=1.0),
                r=["lg", sbk(NR1)], w=["p8"])
            emit("vector", lambda e: e.tensor_mul(p8_s[:], p8_s[:], t8_s[:]),
                 r=["p8", "t8"], w=["p8"])
            emit("vector", lambda e: e.reduce_sum(sb(RS), p8_s[:], axis=AX),
                 r=["p8"], w=[sbk(RS)])
            emit("vector", lambda e: e.reciprocal(sb(RSI), sb(RS)),
                 r=[sbk(RS)], w=[sbk(RSI)])
            emit("vector", lambda e, tt=tt: e.tensor_scalar(
                comb[:, tt * 8:(tt + 1) * 8], p8_s[:], sb(RSI), None, ALU.mult),
                r=["p8", sbk(RSI)], w=["comb"])
        for xp in range(NEXP // 2):
            for exh in range(2):
                ex = xp * 2 + exh
                afk = f"actfm{exh}"
                for jp in range(4):
                    pb = jp % 2
                    emit("gpsimd", lambda e, ex=ex, jp=jp, pb=pb: e.dma_start(
                        out=wgu_g[pb][:], in_=wgu_d[ex, jp, 0]),
                        w=[f"wgug{pb}"], dma=True, lat=9000, cost=350)
                    emit("gpsimd", lambda e, ex=ex, jp=jp, pb=pb: e.dma_start(
                        out=wgu_u[pb][:], in_=wgu_d[ex, jp, 1]),
                        w=[f"wguu{pb}"], dma=True, lat=9000, cost=350)
                    for tt in range(TTB):
                        pg = tt % 2
                        for a in range(NEc):
                            emit("tensor", lambda e, a=a, tt=tt, pb=pb, pg=pg: e.matmul(
                                ps_gg[pg][:, 0:256],
                                y_fm[:, a * TCB + tt * P: a * TCB + (tt + 1) * P],
                                wgu_g[pb][:, a * 256:(a + 1) * 256],
                                start=(a == 0), stop=(a == NEc - 1)),
                                r=["yfm", f"wgug{pb}"], w=[f"psgg{pg}"], cost=140)
                        for a in range(NEc):
                            emit("tensor", lambda e, a=a, tt=tt, pb=pb, pg=pg: e.matmul(
                                ps_gu[pg][:, 0:256],
                                y_fm[:, a * TCB + tt * P: a * TCB + (tt + 1) * P],
                                wgu_u[pb][:, a * 256:(a + 1) * 256],
                                start=(a == 0), stop=(a == NEc - 1)),
                                r=["yfm", f"wguu{pb}"], w=[f"psgu{pg}"], cost=140)
                        emit("scalar", lambda e, pg=pg: e.activation(
                            silu_t[:], ps_gg[pg][:, 0:256], AF.Silu),
                            r=[f"psgg{pg}"], w=["silu"], cost=580)
                        if ex < 8:
                            emit("vector", lambda e, tt=tt, ex=ex, pg=pg: e.scalar_tensor_tensor(
                                act_bt[:], silu_t[:],
                                comb[:, tt * 8 + ex: tt * 8 + ex + 1],
                                ps_gu[pg][:, 0:256], ALU.mult, ALU.mult),
                                r=["silu", "comb", f"psgu{pg}"], w=["actbt"], cost=560)
                        else:
                            emit("vector", lambda e, pg=pg: e.scalar_tensor_tensor(
                                act_bt[:], silu_t[:], 1.0,
                                ps_gu[pg][:, 0:256], ALU.mult, ALU.mult),
                                r=["silu", f"psgu{pg}"], w=["actbt"], cost=560)
                        for j in range(2):
                            emit("tensor", lambda e, j=j, pg=pg: e.transpose(
                                ps_atr[:, pg * 512 + j * P: pg * 512 + (j + 1) * P],
                                act_bt[:, j * P:(j + 1) * P], ident_s[:]),
                                r=["actbt", "ident"], w=[f"psatr{pg}"], cost=180)
                        emit("scalar", lambda e, tt=tt, jp=jp, pg=pg, exh=exh: e.copy(
                            act_fm[:, exh * (TTB * 1024) + tt * 1024 + jp * 256:
                                   exh * (TTB * 1024) + tt * 1024 + jp * 256 + 256],
                            ps_atr[:, pg * 512: pg * 512 + 256]),
                            r=[f"psatr{pg}"], w=[afk], cost=520)
            for os_ in range(4):
                for exh in range(2):
                    emit("gpsimd", lambda e, xp=xp, exh=exh, os_=os_: e.dma_start(
                        out=wd_sl[exh][:], in_=wd_d[xp * 2 + exh, os_]),
                        w=[f"wd{exh}"], dma=True, lat=9000, cost=350)
                for tt in range(TTB):
                    pd = tt % 2
                    for exh in range(2):
                        afk = f"actfm{exh}"
                        for ic in range(8):
                            emit("tensor", lambda e, ic=ic, tt=tt, exh=exh, pd=pd: e.matmul(
                                ps_d[pd][:],
                                act_fm[:, exh * (TTB * 1024) + tt * 1024 + ic * P:
                                       exh * (TTB * 1024) + tt * 1024 + (ic + 1) * P],
                                wd_sl[exh][:, ic * 512:(ic + 1) * 512],
                                start=(exh == 0 and ic == 0),
                                stop=(exh == 1 and ic == 7)),
                                r=[afk, f"wd{exh}"], w=[f"psd{pd}"], cost=270)
                    asl = accB[:, tt * E + os_ * 512: tt * E + (os_ + 1) * 512]
                    if xp == 0:
                        emit("vector", lambda e, asl=asl, pd=pd: e.tensor_copy(
                            asl, ps_d[pd][:]),
                            r=[f"psd{pd}"], w=["accB"], cost=830)
                    else:
                        emit("vector", lambda e, asl=asl, pd=pd: e.tensor_add(
                            asl, asl, ps_d[pd][:]),
                            r=[f"psd{pd}", "accB"], w=["accB"], cost=830)
        for tt in range(TTB):
            pb = tt % 2
            yck = f"yc{(t0 + tt * P) // TCA}"
            emit("sync", lambda e, tt=tt, pb=pb, t0=t0: e.dma_start(
                out=y_f32[pb][:],
                in_=y_d[t0 + tt * P: t0 + (tt + 1) * P, :]),
                r=[yck], w=[f"yf32{pb}"], dma=True, lat=2600)
            sl = y_f32[pb][:]
            asl = accB[:, tt * E:(tt + 1) * E]
            emit("vector", lambda e, sl=sl, asl=asl: e.tensor_add(sl, sl, asl),
                 r=[f"yf32{pb}", "accB"], w=[f"yf32{pb}"], cost=2400)
            emit("vector", lambda e, sl=sl: e.reduce_sum(sb(BMS), sl, axis=AX),
                 r=[f"yf32{pb}"], w=[sbk(BMS)], cost=2400)
            emit("vector", lambda e: e.tensor_scalar(
                sb(BMS), sb(BMS), 1.0 / E, None, ALU.mult),
                r=[sbk(BMS)], w=[sbk(BMS)])
            emit("vector", lambda e, sl=sl: e.tensor_scalar(
                sl, sl, sb(BMS), None, ALU.subtract),
                r=[f"yf32{pb}", sbk(BMS)], w=[f"yf32{pb}"], cost=2400)
            emit("scalar", lambda e, sl=sl, pb=pb: e.activation(
                y_tmb[pb][:], sl, AF.Square, accum_out=sb(BVS)),
                r=[f"yf32{pb}"], w=[f"ytmb{pb}", sbk(BVS)], cost=2600)
            emit("vector", lambda e: e.tensor_scalar(
                sb(BVS), sb(BVS), 1.0 / E, None, ALU.mult),
                r=[sbk(BVS)], w=[sbk(BVS)])
            emit("scalar", lambda e: e.activation(
                sb(BLT), sb(BVS), AF.Ln, bias=sb(BEPS), scale=1.0),
                r=[sbk(BVS), sbk(BEPS)], w=[sbk(BLT)])
            emit("scalar", lambda e: e.activation(
                sb(BLT), sb(BLT), AF.Exp, scale=-0.5),
                r=[sbk(BLT)], w=[sbk(BLT)])
            emit("vector", lambda e, sl=sl: e.tensor_scalar(
                sl, sl, sb(BLT), None, ALU.mult),
                r=[f"yf32{pb}", sbk(BLT)], w=[f"yf32{pb}"], cost=2400)
            emit("sync", lambda e, sl=sl, tt=tt, t0=t0, pb=pb: e.dma_start(
                out=out_d[t0 + tt * P: t0 + (tt + 1) * P, :], in_=sl),
                r=[f"yf32{pb}"], w=["outd"], dma=True, chan=f"out{pb}")

    stB.close()

    with nc.Block() as block:
        s.replay(block)
    top.close()
    return nc


def _shuffle_inputs(ins):
    f32 = np.float32
    W_in = np.ascontiguousarray(ins["L0_attn_W_in"], f32)
    wi_shuf = np.ascontiguousarray(
        W_in.reshape(NEc, P, NO, P).transpose(2, 1, 0, 3).reshape(NO, P, NEc * P))
    b_in_t = np.ascontiguousarray(np.asarray(ins["L0_attn_b_in"], f32).reshape(NO, P).T)
    conv_w_t = np.ascontiguousarray(
        np.asarray(ins["L0_attn_conv_w"], f32).reshape(NO, P, 4).transpose(1, 0, 2).reshape(P, NO * 4))
    conv_b_t = np.ascontiguousarray(np.asarray(ins["L0_attn_conv_b"], f32).reshape(NO, P).T)
    W_out = np.ascontiguousarray(ins["L0_attn_W_out"], f32)
    wo_shuf = np.ascontiguousarray(
        W_out.reshape(16, P, 4, 512).transpose(2, 1, 0, 3).reshape(4, P, 16 * 512))
    bout_bc = np.ascontiguousarray(
        np.broadcast_to(np.asarray(ins["L0_attn_b_out"], f32), (P, E)))
    wg_shuf = np.ascontiguousarray(
        np.asarray(ins["L0_ffn_w_gate"], f32).reshape(NEc, P, 8).transpose(1, 0, 2).reshape(P, NEc * 8))

    wgu = np.empty((NEXP, 4, 2, P, 16 * 256), f32)
    wgu_in = np.asarray(ins["L0_ffn_w_gate_up"], f32)
    for e in range(8):
        q = wgu_in[e].reshape(16, P, 8, 256).transpose(2, 1, 0, 3).reshape(8, P, 16 * 256)
        for jp in range(4):
            wgu[e, jp, 0] = q[jp]
            wgu[e, jp, 1] = q[4 + jp]
    qsh = np.asarray(ins["L0_ffn_w_sh_gate_up"], f32).reshape(
        16, P, 32, 256).transpose(2, 1, 0, 3).reshape(32, P, 16 * 256)
    for p2 in range(4):
        for jp in range(4):
            wgu[8 + p2, jp, 0] = qsh[p2 * 4 + jp]
            wgu[8 + p2, jp, 1] = qsh[16 + p2 * 4 + jp]

    wd = np.empty((NEXP, 4, P, 8 * 512), f32)
    wd_in = np.asarray(ins["L0_ffn_w_down"], f32)
    for e in range(8):
        wd[e] = wd_in[e].reshape(8, P, 4, 512).transpose(2, 1, 0, 3).reshape(4, P, 8 * 512)
    wshd = np.asarray(ins["L0_ffn_w_sh_down"], f32)
    for p2 in range(4):
        wd[8 + p2] = wshd[p2 * 1024:(p2 + 1) * 1024].reshape(
            8, P, 4, 512).transpose(2, 1, 0, 3).reshape(4, P, 8 * 512)

    ident = np.eye(P, dtype=ml_dtypes.bfloat16)
    swap = np.zeros((P, P), f32)
    for p in range(P):
        swap[(p + 64) % P, p] = 1.0
    tri = np.where(np.arange(P)[:, None] >= np.arange(P)[None, :],
                   0.0, NEG).astype(ml_dtypes.bfloat16)
    inv_freq = 1.0 / (10000.0 ** (np.arange(0, D, 2, dtype=np.float32) / D))
    tpos = np.arange(T, dtype=np.float32)
    freqs = np.outer(tpos, inv_freq)
    cosv = np.cos(freqs).astype(f32).T
    sinv = np.sin(freqs).astype(f32).T
    c_cos = np.ascontiguousarray(np.concatenate([cosv, cosv], 0))
    c_sin = np.ascontiguousarray(np.concatenate([-sinv, sinv], 0))

    return {
        "wi_shuf": wi_shuf, "b_in_t": b_in_t, "conv_w_t": conv_w_t,
        "conv_b_t": conv_b_t, "wo_shuf": wo_shuf, "bout_bc": bout_bc,
        "wg_shuf": wg_shuf, "wgu_shuf": wgu, "wd_shuf": wd,
        "c_ident": ident, "c_tri": tri, "c_swap": swap,
        "c_cos": c_cos, "c_sin": c_sin,
    }


def kernel(**inputs):
    shared = _shuffle_inputs(inputs)
    x = np.ascontiguousarray(np.asarray(inputs["L0_attn_x"], np.float32))
    nc = build_program()
    in_maps = []
    for b in range(x.shape[0]):
        m = dict(shared)
        m["x"] = np.ascontiguousarray(x[b, :T])
        in_maps.append(m)
    res = run_bass_kernel_spmd(nc, in_maps, core_ids=list(range(x.shape[0])))
    out = np.stack([np.asarray(r["out"]) for r in res.results], axis=0)
    return out.astype(np.float32)


# version 37
# speedup vs baseline: 1.0303x; 1.0303x over previous
import contextlib

import numpy as np
import ml_dtypes
import concourse.bass as bass
import concourse.mybir as mybir
from concourse.bass_utils import run_bass_kernel_spmd

F32 = mybir.dt.float32
BF16 = mybir.dt.bfloat16
AF = mybir.ActivationFunctionType
ALU = mybir.AluOpType
AX = mybir.AxisListType.X

P = 128
T = 4096
E = 2048
NO = 24
NEc = 16
H = 16
HKV = 4
D = 128
TCA = 512
SCALE = 1.0 / float(np.sqrt(128.0))
NEG = -1.0e30
EPS = 1.0e-5
NEXP = 12

_SERIAL = False
_RESCHED = True
_SAME_ENGINE_EDGES = True
_NO_SKIP = False
_DMA_EDGES = False
_DBG = False


class Sched:

    DEF_COST = {"tensor": 300.0, "vector": 500.0, "scalar": 500.0,
                "gpsimd": 600.0, "sync": 400.0}
    DMA_ISSUE = 220.0
    DMA_LAT = 1400.0

    @staticmethod
    def _eng_of(stream):
        if stream.startswith("gdma"):
            return "gpsimd"
        if stream.startswith("sdma"):
            return "sync"
        return stream

    class Op:
        __slots__ = ("eng", "stream", "idx", "fn", "dma", "deps", "inc",
                     "cum", "waits", "cost", "lat", "prio", "avail",
                     "seq", "succs", "nin")

        def __init__(self, eng, stream, idx, fn, dma, deps, cost, lat, seq):
            self.eng, self.stream, self.idx = eng, stream, idx
            self.fn, self.dma, self.deps = fn, dma, deps
            self.inc = dma
            self.cum = 0
            self.waits = []
            self.cost, self.lat, self.seq = cost, lat, seq
            self.prio = 0.0
            self.avail = 0.0
            self.succs = []
            self.nin = 0

    def __init__(self, nc):
        self.nc = nc
        self.streams = {}
        self.by_eng = {e: [] for e in ("tensor", "vector", "scalar",
                                       "gpsimd", "sync")}
        self.sem = {}
        self.last_w = {}
        self.readers = {}
        self.pending = {e: set() for e in self.by_eng}
        self.anchor = {}
        self.prev = None
        self.order = []

    def emit(self, eng, fn, r=(), w=(), dma=False, cost=None, lat=None,
             chan=None):
        if dma:
            assert eng in ("gpsimd", "sync")
            q = "gdma" if eng == "gpsimd" else "sdma"
            stream = q + ":" + (chan if chan is not None else (w[0] if w else "_"))
        else:
            stream = eng
        if stream not in self.streams:
            self.streams[stream] = []
            self.sem[stream] = self.nc.alloc_semaphore(
                "s_" + stream.replace(":", "_"))
        idx = len(self.streams[stream])
        deps = set()
        for k in r:
            lw = self.last_w.get(k)
            if lw is not None:
                deps.add(lw)
        for k in w:
            lw = self.last_w.get(k)
            if lw is not None:
                deps.add(lw)
            deps |= self.readers.get(k, set())
        eng_real = self._eng_of(stream)
        fence = False
        if self.pending[eng_real]:
            deps |= self.pending[eng_real]
            self.pending[eng_real] = set()
            fence = True
        elif self.anchor.get(eng_real) is not None:
            deps.add(self.anchor[eng_real])
        if _SERIAL and self.prev is not None:
            deps.add(self.prev)
        if cost is None:
            cost = self.DMA_ISSUE if dma else self.DEF_COST[eng_real]
        if lat is None:
            lat = self.DMA_LAT if dma else 0.0
        op = self.Op(eng_real, stream, idx, fn, dma, deps, cost, lat,
                     len(self.order))
        if fence:
            self.anchor[eng_real] = op
        self.prev = op
        self.streams[stream].append(op)
        self.by_eng[eng_real].append(op)
        self.order.append(op)
        for k in r:
            self.readers.setdefault(k, set()).add(op)
        for k in w:
            self.last_w[k] = op
            self.readers[k] = set()
        return op

    def barrier(self):
        snap = set(self.order)
        for e in self.pending:
            self.pending[e] = set(snap)
        self.anchor = {}

    def _reorder(self):
        import heapq
        ops = self.order
        for op in ops:
            op.succs = []
        for op in ops:
            op.nin = len(op.deps)
            for d in op.deps:
                d.succs.append(op)
        for op in reversed(ops):
            m = 0.0
            for s in op.succs:
                if s.prio > m:
                    m = s.prio
            op.prio = op.cost + op.lat + m
        engines = list(self.by_eng.keys())
        avail_h = {e: [] for e in engines}
        ready_h = {e: [] for e in engines}
        wake = []
        free = {e: 0.0 for e in engines}

        def release(op):
            heapq.heappush(avail_h[op.eng], (op.avail, op.seq, op))
            heapq.heappush(wake, (op.avail, op.seq, op.eng))

        for op in ops:
            if op.nin == 0:
                release(op)
        new_order = []
        n = len(ops)
        while len(new_order) < n:
            t, _, e = heapq.heappop(wake)
            t = max(t, free[e])
            ah, rh = avail_h[e], ready_h[e]
            while ah and ah[0][0] <= t:
                _, sq, op = heapq.heappop(ah)
                heapq.heappush(rh, (-op.prio, sq, op))
            if not rh:
                if ah:
                    heapq.heappush(wake, (max(ah[0][0], free[e]), ah[0][1], e))
                continue
            _, _, op = heapq.heappop(rh)
            start = t
            busy = self.DMA_ISSUE if op.dma else op.cost
            free[e] = start + busy
            data_t = start + op.cost + op.lat
            new_order.append(op)
            for s in op.succs:
                dt_s = data_t if s.eng == op.eng and not op.dma else data_t + 250.0
                if dt_s > s.avail:
                    s.avail = dt_s
                s.nin -= 1
                if s.nin == 0:
                    release(s)
            if ah or rh:
                nt = free[e]
                if not rh and ah and ah[0][0] > nt:
                    nt = ah[0][0]
                heapq.heappush(wake, (nt, op.seq, e))
        self.order = new_order
        for st in self.streams:
            self.streams[st] = []
        for e in self.by_eng:
            self.by_eng[e] = []
        for op in new_order:
            op.idx = len(self.streams[op.stream])
            self.streams[op.stream].append(op)
            self.by_eng[op.eng].append(op)

    def _keep_edge(self, dep_op, op):
        if _NO_SKIP:
            return True
        if dep_op.stream == op.stream:
            if dep_op.dma and op.dma:
                return False
            if op.stream == "tensor":
                return False
        if (not _SAME_ENGINE_EDGES and dep_op.eng == op.eng
                and not dep_op.dma and dep_op.stream != op.stream):
            return False
        return True

    def finalize(self):
        if _RESCHED and not _SERIAL:
            self._reorder()
        for ops in self.by_eng.values():
            for op in ops:
                best = {}
                for dep_op in op.deps:
                    if (self._keep_edge(dep_op, op)
                            and dep_op.idx > best.get(dep_op.stream, (-1, None))[0]):
                        best[dep_op.stream] = (dep_op.idx, dep_op)
                kept = [v[1] for v in best.values()]
                for dep_op in kept:
                    dep_op.inc = True
                op.deps = kept
        if _SERIAL:
            for ops in self.streams.values():
                for op in ops:
                    op.inc = True
        for ops in self.streams.values():
            c = 0
            for op in ops:
                if op.inc:
                    c += 16 if op.dma else 1
                op.cum = c
        for eng, ops in self.by_eng.items():
            hwm = {}
            for op in ops:
                need = {}
                for dep_op in op.deps:
                    st = dep_op.stream
                    v = dep_op.cum
                    if v > need.get(st, -1):
                        need[st] = v
                for st, v in sorted(need.items()):
                    if v > hwm.get(st, 0):
                        op.waits.append((self.sem[st], v))
                        hwm[st] = v

    def totals(self):
        out = {}
        for st, ops in self.streams.items():
            out[st] = ops[-1].cum if ops else 0
        return out

    def replay(self, block):
        nc = self.nc
        self.finalize()
        tot = self.totals()
        engobj = {"tensor": nc.tensor, "vector": nc.vector,
                  "scalar": nc.scalar, "gpsimd": nc.gpsimd, "sync": nc.sync}

        def body(eng):
            e = engobj[eng]
            for op in self.by_eng[eng]:
                for (sem, v) in op.waits:
                    e.wait_ge(sem, v)
                inst = op.fn(e)
                if op.inc:
                    inst.then_inc(self.sem[op.stream], 16 if op.dma else 1)
            if eng == "sync":
                for st in sorted(tot):
                    if tot[st]:
                        e.wait_ge(self.sem[st], tot[st])

        block.tensor(lambda _: body("tensor"))
        block.vector(lambda _: body("vector"))
        block.scalar(lambda _: body("scalar"))
        block.gpsimd(lambda _: body("gpsimd"))
        block.sync(lambda _: body("sync"))


def build_program():
    NCA = T // TCA
    TCB = min(1024, T)
    NCB = T // TCB
    TTB = TCB // P

    nc = bass.Bass("TRN2", target_bir_lowering=False)
    dt = nc.dram_tensor

    x_d = dt("x", [T, E], F32, kind="ExternalInput").ap()
    wi_d = dt("wi_shuf", [NO, P, NEc * P], F32, kind="ExternalInput").ap()
    bin_d = dt("b_in_t", [P, NO], F32, kind="ExternalInput").ap()
    cw_d = dt("conv_w_t", [P, NO * 4], F32, kind="ExternalInput").ap()
    cb_d = dt("conv_b_t", [P, NO], F32, kind="ExternalInput").ap()
    wo_d = dt("wo_shuf", [4, P, 16 * 512], F32, kind="ExternalInput").ap()
    bout_d = dt("bout_bc", [P, E], F32, kind="ExternalInput").ap()
    wg_d = dt("wg_shuf", [P, NEc * 8], F32, kind="ExternalInput").ap()
    wgu_d = dt("wgu_shuf", [NEXP, 4, 2, P, 16 * 256], F32, kind="ExternalInput").ap()
    wd_d = dt("wd_shuf", [NEXP, 4, P, 8 * 512], F32, kind="ExternalInput").ap()
    ident_d = dt("c_ident", [P, P], BF16, kind="ExternalInput").ap()
    tri_d = dt("c_tri", [P, P], BF16, kind="ExternalInput").ap()
    swap_d = dt("c_swap", [P, P], F32, kind="ExternalInput").ap()
    cos_d = dt("c_cos", [P, T], F32, kind="ExternalInput").ap()
    sin_d = dt("c_sin", [P, T], F32, kind="ExternalInput").ap()

    y_d = dt("y_scr", [T, E], F32,
             kind="ExternalOutput" if _DBG else "Internal").ap()
    out_d = dt("out", [T, E], F32, kind="ExternalOutput").ap()

    s = Sched(nc)
    emit = s.emit

    top = contextlib.ExitStack()

    def SBt(st, name, shape, dtyp):
        return st.enter_context(nc.sbuf_tensor(name, shape, dtyp))

    def PSt(st, name, shape, dtyp=F32):
        return st.enter_context(nc.psum_tensor(name, shape, dtyp))

    ident_s = SBt(top, "ident_s", [P, P], BF16)
    emit("sync", lambda e: e.dma_start(out=ident_s[:], in_=ident_d),
         w=["ident"], dma=True)

    stA = contextlib.ExitStack()
    Ksb = SBt(stA, "Ksb", [P, HKV * T], BF16)
    Vsb = SBt(stA, "Vsb", [P, (T // P) * 512], BF16)
    wi_sl = [SBt(stA, f"wi_sl{i}", [P, NEc * P], BF16) for i in range(2)]
    wo_sl = [SBt(stA, f"wo_sl{i}", [P, 8 * 512], BF16) for i in range(2)]
    x_bf = SBt(stA, "x_bf", [P, E], BF16)
    x_fm = SBt(stA, "x_fm", [P, NEc * TCA], BF16)
    x_f32 = SBt(stA, "x_f32", [P, 4 * E], F32)
    Qsb = SBt(stA, "Qsb", [P, H * TCA], BF16)
    ctxfm = SBt(stA, "ctxfm", [P, H * TCA], BF16)
    qkv_o = [SBt(stA, f"qkv_o{i}", [P, 516], F32) for i in range(2)]
    conv_o = [SBt(stA, f"conv_o{i}", [P, TCA], F32) for i in range(2)]
    rot_t = [SBt(stA, f"rot_t{i}", [P, TCA], F32) for i in range(2)]
    rot2_t = [SBt(stA, "rot2_t0", [P, TCA], F32)] * 2
    swap_s = SBt(stA, "swap_s", [P, P], F32)
    cos_s = SBt(stA, "cos_s", [P, TCA], F32)
    sin_s = SBt(stA, "sin_s", [P, TCA], F32)
    probs_bf = [SBt(stA, f"probs_bf{i}", [P, 1024], BF16) for i in range(2)]
    pT_sb = [SBt(stA, f"pT_sb{i}", [P, 1024], BF16) for i in range(2)]
    acc_fl = [SBt(stA, f"acc_fl{i}", [P, P], F32) for i in range(8)]
    ctx_bt = [SBt(stA, f"ctx_bt{i}", [P, P], BF16) for i in range(8)]
    v_bf = SBt(stA, "v_bf", [P, TCA], BF16)
    bin_s = SBt(stA, "bin_s", [P, NO], F32)
    cw_s = SBt(stA, "cw_s", [P, NO * 4], F32)
    cb_s = SBt(stA, "cb_s", [P, NO], F32)
    tri_s = SBt(stA, "tri_s", [P, P], BF16)
    B3 = SBt(stA, "B3", [P, NO * 4], F32)
    smA = SBt(stA, "smA", [P, 100], F32)
    ps_all = PSt(stA, "ps_all", [P, 4 * 512])
    ps_pTl = [PSt(stA, f"ps_pT{i}", [P, 1024], BF16) for i in range(2)]
    ps_pT = ps_pTl[0]
    ps_ctxl = [PSt(stA, f"ps_ctx{i}", [P, 512]) for i in range(2)]

    def psf(i):
        return ps_all[:, i * 512:(i + 1) * 512], f"psc{i}"

    MSUM, VSUM, LNT, EPSV = range(4)
    LRUN, LINV, CORR0, CORR1 = range(4)
    NMH0, SSH0 = 4, 8
    LSC0 = NMH0

    def sv(i):
        return smA[:, i:i + 1]

    def svk(i):
        return f"smA{i}"

    def fv(sl, i):
        return smA[:, 4 + sl * 12 + i: 5 + sl * 12 + i]

    def fvr(sl, i, n):
        return smA[:, 4 + sl * 12 + i: 4 + sl * 12 + i + n]

    def fvk(sl, i):
        return f"smF{sl}_{i}"

    emit("sync", lambda e: e.dma_start(out=bin_s[:], in_=bin_d), w=["bin"], dma=True)
    emit("sync", lambda e: e.dma_start(out=cw_s[:], in_=cw_d), w=["cw"], dma=True)
    emit("sync", lambda e: e.dma_start(out=cb_s[:], in_=cb_d), w=["cb"], dma=True)
    emit("sync", lambda e: e.dma_start(out=tri_s[:], in_=tri_d), w=["tri"], dma=True)
    emit("sync", lambda e: e.dma_start(out=swap_s[:], in_=swap_d), w=["swap"], dma=True)
    emit("gpsimd", lambda e: e.memset(B3[:], 0.0), w=["B3"])
    emit("gpsimd", lambda e: e.memset(sv(EPSV), EPS), w=[svk(EPSV)])

    for c in range(NCA):
        t0 = c * TCA
        for r in range(4):
            src = x_d[t0 + r * P: t0 + (r + 1) * P, :]
            emit("gpsimd", lambda e, src=src: e.dma_start(
                out=x_bf[:, :], in_=src),
                w=["xbf"], dma=True, lat=7000, cost=350)
            for eh in range(NEc // 2):
                for tl in range(2):
                    ec = eh * 2 + tl
                    emit("tensor", lambda e, ec=ec, tl=tl: e.transpose(
                        ps_pT[:, tl * P:(tl + 1) * P],
                        x_bf[:, ec * P:(ec + 1) * P],
                        ident_s[:]),
                        r=["xbf", "ident"], w=["pspT0"], cost=180)
                emit("scalar", lambda e, eh=eh, r=r: e.copy(
                    x_fm[:, :].rearrange("p (a b) -> p a b", a=NEc)[
                        :, 2 * eh:2 * eh + 2, r * P:(r + 1) * P],
                    ps_pT[:, 0:256].rearrange("p (a b) -> p a b", a=2)),
                    r=["pspT0"], w=["xfm"])
        emit("sync", lambda e, c=c: e.dma_start(
            out=cos_s[:], in_=cos_d[:, c * TCA:(c + 1) * TCA]),
            r=["cosuse"], w=["cos"], dma=True)
        emit("sync", lambda e, c=c: e.dma_start(
            out=sin_s[:], in_=sin_d[:, c * TCA:(c + 1) * TCA]),
            r=["sinuse"], w=["sin"], dma=True)

        for o in range(NO):
            pb = o % 2
            emit("gpsimd", lambda e, o=o, pb=pb: e.dma_start(
                out=wi_sl[pb][:], in_=wi_d[o]), w=[f"wi{pb}"], dma=True,
                lat=9000, cost=350)
            for a in range(NEc):
                emit("tensor", lambda e, a=a, pb=pb: e.matmul(
                    psf(2 * pb)[0], wi_sl[pb][:, a * P:(a + 1) * P],
                    x_fm[:, a * TCA:(a + 1) * TCA],
                    start=(a == 0), stop=(a == NEc - 1)),
                    r=[f"wi{pb}", "xfm"], w=[f"psc{2 * pb}"], cost=360)
            emit("scalar", lambda e, o=o, pb=pb: e.activation(
                qkv_o[pb][:, 3:515], psf(2 * pb)[0], AF.Identity,
                bias=bin_s[:, o:o + 1], scale=1.0),
                r=[f"psc{2 * pb}", "bin"], w=[f"qkv{pb}"], cost=830)
            emit("vector", lambda e, o=o, pb=pb: e.tensor_copy(
                qkv_o[pb][:, 0:3], B3[:, o * 4: o * 4 + 3]),
                r=["B3"], w=[f"qkv{pb}"])
            emit("vector", lambda e, o=o, pb=pb: e.tensor_copy(
                B3[:, o * 4: o * 4 + 3], qkv_o[pb][:, 512:515]),
                r=[f"qkv{pb}"], w=["B3"])
            ceng = "vector"
            emit(ceng, lambda e, o=o, pb=pb: e.tensor_scalar(
                conv_o[pb][:], qkv_o[pb][:, 0:512],
                cw_s[:, o * 4: o * 4 + 1], cb_s[:, o:o + 1],
                ALU.mult, ALU.add),
                r=[f"qkv{pb}", "cw", "cb"], w=[f"conv{pb}"], cost=900)
            for k in range(1, 4):
                emit(ceng, lambda e, o=o, pb=pb, k=k: e.scalar_tensor_tensor(
                    conv_o[pb][:], qkv_o[pb][:, k:k + 512],
                    cw_s[:, o * 4 + k: o * 4 + k + 1],
                    conv_o[pb][:], ALU.mult, ALU.add),
                    r=[f"qkv{pb}", "cw", f"conv{pb}"], w=[f"conv{pb}"], cost=900)
            if o < 20:
                emit("tensor", lambda e, pb=pb: e.matmul(
                    psf(1)[0], swap_s[:], conv_o[pb][:],
                    start=True, stop=True),
                    r=["swap", f"conv{pb}"], w=["psc1"], cost=1150)
                emit("gpsimd", lambda e, pb=pb: e.tensor_mul(
                    rot_t[pb][:], conv_o[pb][:], cos_s[:]),
                    r=[f"conv{pb}", "cos", "cosuse"], w=[f"rott{pb}"], cost=900)
                emit("vector", lambda e, pb=pb: e.tensor_mul(
                    rot2_t[pb][:], psf(1)[0], sin_s[:]),
                    r=["psc1", "sin", "sinuse"], w=["rot2t"], cost=830)
                if o < 16:
                    dst = Qsb[:, o * TCA:(o + 1) * TCA]
                    dkey = "Q"
                else:
                    dst = Ksb[:, (o - 16) * T + t0: (o - 16) * T + t0 + TCA]
                    dkey = f"Kc{c}"
                emit("gpsimd", lambda e, dst=dst, pb=pb: e.tensor_add(
                    dst, rot_t[pb][:], rot2_t[pb][:]),
                    r=[f"rott{pb}", "rot2t"], w=[dkey], cost=700)
            else:
                kv = o - 20
                emit("vector", lambda e, pb=pb: e.tensor_copy(
                    v_bf[:], conv_o[pb][:]),
                    r=[f"conv{pb}"], w=["vbf"], cost=560)
                for j in range(4):
                    emit("tensor", lambda e, j=j: e.transpose(
                        ps_pT[:, j * P:(j + 1) * P],
                        v_bf[:, j * P:(j + 1) * P], ident_s[:]),
                        r=["vbf", "ident"], w=["pspT0"], cost=180)
                for j in range(4):
                    kt = 4 * c + j
                    emit("scalar", lambda e, kt=kt, kv=kv, j=j: e.copy(
                        Vsb[:, kt * 512 + kv * P: kt * 512 + (kv + 1) * P],
                        ps_pT[:, j * P:(j + 1) * P]),
                        r=["pspT0"], w=[f"Vc{c}"])

        GK = 8

        def scfull(lane, span):
            return ps_all[:, lane * 1024: lane * 1024 + span]

        def sckeys(lane, span):
            ks = [f"psc{lane * 2}"]
            if span > 512:
                ks.append(f"psc{lane * 2 + 1}")
            return ks

        def fl_span(gi, g):
            return min(GK, g + 1 - GK * gi) * P

        def fl_scores(lane, h, qt, gi, g):
            kv = h // 4
            kb0 = GK * gi
            nb = min(GK, g + 1 - kb0)
            span = nb * P
            diag = (kb0 + nb - 1 == g)
            kck = [f"Kc{i}" for i in range(kb0 // 4, (kb0 + nb - 1) // 4 + 1)]
            if span > 512:
                parts = [(0, 512, False), (512, span, diag)]
            else:
                parts = [(0, span, diag)]
            for (c0, c1, dg) in parts:
                key = f"psc{lane * 2 + c0 // 512}"
                emit("tensor",
                     lambda e, h=h, qt=qt, kv=kv, kb0=kb0, c0=c0, c1=c1,
                     lane=lane, dg=dg:
                     e.matmul(ps_all[:, lane * 1024 + c0: lane * 1024 + c1],
                              Qsb[:, h * TCA + qt * P: h * TCA + (qt + 1) * P],
                              Ksb[:, kv * T + kb0 * P + c0: kv * T + kb0 * P + c1],
                              start=True, stop=not dg),
                     r=["Q"] + kck, w=[key], cost=(c1 - c0) * 0.7 + 120)
                if dg:
                    emit("tensor", lambda e, lane=lane, c1=c1: e.matmul(
                        ps_all[:, lane * 1024 + c1 - P: lane * 1024 + c1],
                        ident_s[:], tri_s[:],
                        start=False, stop=True),
                        r=["ident", "tri"], w=[key], cost=180)

        def fl_gmax(lane, sl, gi, g):
            span = fl_span(gi, g)
            emit("vector", lambda e, span=span, sl=sl, lane=lane: e.reduce_max(
                fv(sl, LINV), scfull(lane, span), axis=AX),
                r=sckeys(lane, span), w=[fvk(sl, LINV)], cost=span * 1.35 + 250)

        def fl_nm(sl, gi):
            if gi == 0:
                emit("vector", lambda e, sl=sl: e.tensor_scalar(
                    fv(sl, NMH0), fv(sl, LINV), -SCALE, None, ALU.mult),
                    r=[fvk(sl, LINV)], w=[fvk(sl, NMH0)], cost=220)
            else:
                emit("vector", lambda e, sl=sl, gi=gi: e.tensor_scalar(
                    fv(sl, NMH0 + gi), fv(sl, LINV), -SCALE,
                    fv(sl, NMH0 + gi - 1), ALU.mult, ALU.min),
                    r=[fvk(sl, LINV), fvk(sl, NMH0 + gi - 1)],
                    w=[fvk(sl, NMH0 + gi)], cost=220)

        def fl_corr(sl, gi):
            cur = gi % 2
            emit("scalar", lambda e, sl=sl, gi=gi, cur=cur: e.activation(
                fv(sl, CORR0 + cur), fv(sl, NMH0 + gi - 1), AF.Exp,
                bias=fv(sl, NMH0 + gi), scale=-1.0),
                r=[fvk(sl, NMH0 + gi - 1), fvk(sl, NMH0 + gi)],
                w=[fvk(sl, CORR0 + cur)], cost=240)

        def fl_exp(lane, sl, gi, g):
            span = fl_span(gi, g)
            emit("scalar", lambda e, span=span, lane=lane, sl=sl, gi=gi:
                 e.activation(
                     probs_bf[lane][:, 0:span], scfull(lane, span), AF.Exp,
                     bias=fv(sl, NMH0 + gi), scale=SCALE,
                     accum_out=fv(sl, SSH0 + gi)),
                 r=sckeys(lane, span) + [fvk(sl, NMH0 + gi)],
                 w=[f"probs{lane}", fvk(sl, SSH0 + gi)],
                 cost=span * 1.15 + 220)

        def fl_transp(lane, gi, g):
            span = fl_span(gi, g)
            for j in range(span // P):
                emit("tensor", lambda e, j=j, lane=lane: e.transpose(
                    ps_pTl[lane][:, j * P:(j + 1) * P],
                    probs_bf[lane][:, j * P:(j + 1) * P], ident_s[:]),
                    r=[f"probs{lane}", "ident"], w=[f"pspT{lane}"], cost=180)

        def fl_evac(lane, gi, g):
            span = fl_span(gi, g)
            h1 = min(span, 512)
            emit("scalar", lambda e, h1=h1, lane=lane: e.copy(
                pT_sb[lane][:, 0:h1], ps_pTl[lane][:, 0:h1]),
                r=[f"pspT{lane}"], w=[f"pTsb{lane}"], cost=h1 * 1.1 + 200)
            if span > 512:
                emit("vector", lambda e, span=span, lane=lane: e.tensor_copy(
                    pT_sb[lane][:, 512:span], ps_pTl[lane][:, 512:span]),
                    r=[f"pspT{lane}"], w=[f"pTsb{lane}"],
                    cost=(span - 512) * 0.6 + 220)

        def fl_ctx(lane, h, qt, gi, g):
            kv = h // 4
            kb0 = GK * gi
            nb = fl_span(gi, g) // P
            vck = [f"Vc{i}" for i in range(kb0 // 4, (kb0 + nb - 1) // 4 + 1)]
            for j in range(nb):
                kt = kb0 + j
                emit("tensor", lambda e, j=j, kt=kt, kv=kv, nb=nb, lane=lane:
                     e.matmul(ps_ctxl[lane][:, 0:P],
                              pT_sb[lane][:, j * P:(j + 1) * P],
                              Vsb[:, kt * 512 + kv * P: kt * 512 + (kv + 1) * P],
                              start=(j == 0), stop=(j == nb - 1)),
                     r=[f"pTsb{lane}"] + vck, w=[f"psctx{lane}"], cost=180)

        def fl_acc(lane, sl, gi):
            if gi == 0:
                emit("scalar", lambda e, lane=lane, sl=sl: e.copy(
                    acc_fl[sl][:], ps_ctxl[lane][:, 0:P]),
                    r=[f"psctx{lane}"], w=[f"accfl{sl}"], cost=360)
            else:
                cur = gi % 2
                emit("vector", lambda e, lane=lane, sl=sl, cur=cur:
                     e.scalar_tensor_tensor(
                         acc_fl[sl][:], acc_fl[sl][:], fv(sl, CORR0 + cur),
                         ps_ctxl[lane][:, 0:P], ALU.mult, ALU.add),
                     r=[f"accfl{sl}", fvk(sl, CORR0 + cur), f"psctx{lane}"],
                     w=[f"accfl{sl}"], cost=420)

        def flash_final(h, qt, lane, sl, ng):
            nmk = [fvk(sl, NMH0 + i) for i in range(ng)]
            ssk = [fvk(sl, SSH0 + i) for i in range(ng)]
            lsck = f"lsc{sl}"
            emit("scalar", lambda e, sl=sl, ng=ng: e.activation(
                fvr(sl, LSC0, ng), fvr(sl, NMH0, ng), AF.Exp,
                bias=fv(sl, NMH0 + ng - 1), scale=-1.0),
                r=nmk, w=[lsck] + nmk, cost=240)
            emit("vector", lambda e, sl=sl, ng=ng: e.tensor_mul(
                fvr(sl, LSC0, ng), fvr(sl, LSC0, ng), fvr(sl, SSH0, ng)),
                r=[lsck] + ssk, w=[lsck], cost=220)
            emit("vector", lambda e, sl=sl, ng=ng: e.reduce_sum(
                fv(sl, LRUN), fvr(sl, LSC0, ng), axis=AX),
                r=[lsck], w=[fvk(sl, LRUN)], cost=220)
            emit("vector", lambda e, sl=sl: e.reciprocal(
                fv(sl, LINV), fv(sl, LRUN)),
                r=[fvk(sl, LRUN)], w=[fvk(sl, LINV)], cost=220)
            emit("vector", lambda e, sl=sl: e.tensor_scalar(
                ctx_bt[sl][:], acc_fl[sl][:], fv(sl, LINV), None, ALU.mult),
                r=[f"accfl{sl}", fvk(sl, LINV)], w=[f"ctxbt{sl}"], cost=360)
            emit("tensor", lambda e, lane=lane, sl=sl: e.transpose(
                ps_pTl[lane][:, 0:P], ctx_bt[sl][:], ident_s[:]),
                r=[f"ctxbt{sl}", "ident"], w=[f"pspT{lane}"], cost=180)
            emit("scalar", lambda e, h=h, qt=qt, lane=lane: e.copy(
                ctxfm[:, h * TCA + qt * P: h * TCA + (qt + 1) * P],
                ps_pTl[lane][:, 0:P]),
                r=[f"pspT{lane}"], w=["ctxfm"], cost=440)

        iters = [(h, qt) for h in range(H) for qt in range(4)]
        for i0 in range(0, len(iters), 2):
            pair = iters[i0:i0 + 2]
            pp = (i0 // 2) % 4
            gs = [4 * c + qt for (_, qt) in pair]
            ngs = [(g + GK) // GK for g in gs]
            for gi in range(max(ngs) + 1):
                curw = [(lane, h, qt) for lane, (h, qt) in enumerate(pair)
                        if gi < ngs[lane]]
                tailw = [(lane, h, qt) for lane, (h, qt) in enumerate(pair)
                         if 0 <= gi - 1 < ngs[lane]]
                for (lane, h, qt) in curw:
                    fl_scores(lane, h, qt, gi, gs[lane])
                for (lane, h, qt) in tailw:
                    fl_transp(lane, gi - 1, gs[lane])
                for (lane, h, qt) in tailw:
                    fl_evac(lane, gi - 1, gs[lane])
                for (lane, h, qt) in tailw:
                    fl_ctx(lane, h, qt, gi - 1, gs[lane])
                for (lane, h, qt) in curw:
                    fl_gmax(lane, lane + 2 * pp, gi, gs[lane])
                for (lane, h, qt) in curw:
                    fl_nm(lane + 2 * pp, gi)
                for (lane, h, qt) in tailw:
                    fl_acc(lane, lane + 2 * pp, gi - 1)
                for (lane, h, qt) in curw:
                    if gi >= 1:
                        fl_corr(lane + 2 * pp, gi)
                for (lane, h, qt) in curw:
                    fl_exp(lane, lane + 2 * pp, gi, gs[lane])
            for lane, (h, qt) in enumerate(pair):
                flash_final(h, qt, lane, lane + 2 * pp, ngs[lane])

        for tt in range(4):
            emit("sync", lambda e, tt=tt, t0=t0: e.dma_start(
                out=x_f32[:, tt * E:(tt + 1) * E],
                in_=x_d[t0 + tt * P: t0 + (tt + 1) * P, :]),
                w=[f"xf32t{tt}"], dma=True, chan=f"xf32t{tt}", lat=2600)
        for tt in range(4):
            emit("gpsimd", lambda e, tt=tt: e.dma_start(
                out=x_f32[:, tt * E:(tt + 1) * E], in_=bout_d,
                accum_op=ALU.add),
                r=[f"xf32t{tt}"], w=[f"xf32t{tt}"], dma=True,
                chan=f"xfa{tt}")
        for og in range(4):
            for fh in range(2):
                emit("gpsimd", lambda e, og=og, fh=fh: e.dma_start(
                    out=wo_sl[fh][:],
                    in_=wo_d[og, :, fh * 4096:(fh + 1) * 4096]),
                    w=[f"wo{fh}"], dma=True, lat=14000, cost=400)
                for f8 in range(8):
                    f = fh * 8 + f8
                    for tt in range(4):
                        emit("tensor",
                             lambda e, f=f, f8=f8, fh=fh, tt=tt:
                             e.matmul(psf(tt)[0],
                                      ctxfm[:, f * TCA + tt * P: f * TCA + (tt + 1) * P],
                                      wo_sl[fh][:, f8 * 512:(f8 + 1) * 512],
                                      start=(f == 0), stop=(f == 15)),
                             r=["ctxfm", f"wo{fh}"], w=[f"psc{tt}"], cost=360)
            for tt in range(4):
                emit("vector", lambda e, tt=tt, og=og: e.tensor_add(
                    x_f32[:, tt * E + og * 512: tt * E + (og + 1) * 512],
                    x_f32[:, tt * E + og * 512: tt * E + (og + 1) * 512],
                    psf(tt)[0]),
                    r=[f"xf32t{tt}", f"psc{tt}"], w=[f"xf32t{tt}"], cost=830)
        for tt in range(4):
            sl = x_f32[:, tt * E:(tt + 1) * E]
            emit("vector", lambda e, sl=sl: e.reduce_sum(sv(MSUM), sl, axis=AX),
                 r=[f"xf32t{tt}"], w=[svk(MSUM)], cost=2400)
            emit("vector", lambda e: e.tensor_scalar(
                sv(MSUM), sv(MSUM), 1.0 / E, None, ALU.mult),
                r=[svk(MSUM)], w=[svk(MSUM)])
            emit("vector", lambda e, sl=sl: e.tensor_scalar(
                sl, sl, sv(MSUM), None, ALU.subtract),
                r=[f"xf32t{tt}", svk(MSUM)], w=[f"xf32t{tt}"], cost=2400)
            emit("scalar", lambda e, sl=sl: e.activation(
                x_bf[:, 0:E], sl, AF.Square, accum_out=sv(VSUM)),
                r=[f"xf32t{tt}"], w=["xbf", svk(VSUM)], cost=2600)
            emit("vector", lambda e: e.tensor_scalar(
                sv(VSUM), sv(VSUM), 1.0 / E, None, ALU.mult),
                r=[svk(VSUM)], w=[svk(VSUM)])
            emit("scalar", lambda e: e.activation(
                sv(LNT), sv(VSUM), AF.Ln, bias=sv(EPSV), scale=1.0),
                r=[svk(VSUM), svk(EPSV)], w=[svk(LNT)])
            emit("scalar", lambda e: e.activation(
                sv(LNT), sv(LNT), AF.Exp, scale=-0.5),
                r=[svk(LNT)], w=[svk(LNT)])
            emit("vector", lambda e, sl=sl: e.tensor_scalar(
                sl, sl, sv(LNT), None, ALU.mult),
                r=[f"xf32t{tt}", svk(LNT)], w=[f"xf32t{tt}"], cost=2400)
            emit("sync", lambda e, sl=sl, tt=tt, t0=t0: e.dma_start(
                out=y_d[t0 + tt * P: t0 + (tt + 1) * P, :], in_=sl),
                r=[f"xf32t{tt}"], w=[f"yc{c}"], dma=True, chan=f"y{tt}")

    stA.close()
    s.barrier()

    stB = contextlib.ExitStack()
    accB = SBt(stB, "accB", [P, TTB * E], F32)
    y_fm = SBt(stB, "y_fm", [P, NEc * TCB], BF16)
    act_fm = SBt(stB, "act_fm", [P, 2 * TTB * 1024], BF16)
    wgu_g = [SBt(stB, f"wgu_g{i}", [P, 16 * 256], BF16) for i in range(2)]
    wgu_u = [SBt(stB, f"wgu_u{i}", [P, 16 * 256], BF16) for i in range(2)]
    wd_sl = [SBt(stB, f"wd_sl{i}", [P, 8 * 512], BF16) for i in range(2)]
    y_tmb = [SBt(stB, f"y_tmb{i}", [P, E], BF16) for i in range(2)]
    y_f32 = [SBt(stB, f"y_f32{i}", [P, E], F32) for i in range(2)]
    silu_t = SBt(stB, "silu_t", [P, 256], F32)
    act_bt = SBt(stB, "act_bt", [P, 256], BF16)
    wg_s = SBt(stB, "wg_s", [P, NEc * 8], BF16)
    lg_s = SBt(stB, "lg_s", [P, 8], F32)
    p8_s = SBt(stB, "p8_s", [P, 8], F32)
    t8_s = SBt(stB, "t8_s", [P, 8], F32)
    comb = SBt(stB, "comb", [P, TTB * 8], F32)
    smB = SBt(stB, "smB", [P, 16], F32)
    ps_gg = [PSt(stB, f"ps_gg{i}", [P, 512]) for i in range(2)]
    ps_gu = [PSt(stB, f"ps_gu{i}", [P, 512]) for i in range(2)]
    ps_atr = PSt(stB, "ps_atr", [P, 1024], BF16)
    ps_d = [PSt(stB, f"ps_d{i}", [P, 512]) for i in range(2)]
    ps_ytr = PSt(stB, "ps_ytr", [P, E // 2], BF16)

    R1, NR1, R2, RS, RSI, BMS, BVS, BLT, BEPS = range(9)

    def sb(i):
        return smB[:, i:i + 1]

    def sbk(i):
        return f"smB{i}"

    emit("gpsimd", lambda e: e.dma_start(out=wg_s[:], in_=wg_d),
         w=["wg"], dma=True)
    emit("gpsimd", lambda e: e.memset(sb(BEPS), EPS), w=[sbk(BEPS)])

    for cb in range(NCB):
        t0 = cb * TCB
        for tt in range(TTB):
            pb = tt % 2
            yck = f"yc{(t0 + tt * P) // TCA}"
            emit("gpsimd", lambda e, tt=tt, pb=pb, t0=t0: e.dma_start(
                out=y_tmb[pb][:],
                in_=y_d[t0 + tt * P: t0 + (tt + 1) * P, :]),
                r=[yck], w=[f"ytmb{pb}"], dma=True, lat=2200)
            for half in range(2):
                for eh in range(NEc // 2):
                    ec = half * (NEc // 2) + eh
                    emit("tensor", lambda e, ec=ec, eh=eh, pb=pb: e.transpose(
                        ps_ytr[:, eh * P:(eh + 1) * P],
                        y_tmb[pb][:, ec * P:(ec + 1) * P], ident_s[:]),
                        r=[f"ytmb{pb}", "ident"], w=["psytr"], cost=180)
                emit("vector", lambda e, tt=tt, half=half: e.tensor_copy(
                    y_fm[:, :].rearrange("p (a b) -> p a b", a=NEc)[
                        :, half * (NEc // 2):(half + 1) * (NEc // 2),
                        tt * P:(tt + 1) * P],
                    ps_ytr[:, :].rearrange("p (a b) -> p a b", a=NEc // 2)),
                    r=["psytr"], w=["yfm"], cost=700)
        for tt in range(TTB):
            for a in range(NEc):
                emit("tensor", lambda e, a=a, tt=tt: e.matmul(
                    ps_d[0][:, 0:8], y_fm[:, a * TCB + tt * P: a * TCB + (tt + 1) * P],
                    wg_s[:, a * 8:(a + 1) * 8],
                    start=(a == 0), stop=(a == NEc - 1)),
                    r=["yfm", "wg"], w=["psd0"], cost=65)
            emit("vector", lambda e: e.tensor_copy(lg_s[:], ps_d[0][:, 0:8]),
                 r=["psd0"], w=["lg"])
            emit("vector", lambda e: e.reduce_max(sb(R1), lg_s[:], axis=AX),
                 r=["lg"], w=[sbk(R1)])
            emit("vector", lambda e: e.tensor_scalar(
                sb(NR1), sb(R1), -1.0, None, ALU.mult),
                r=[sbk(R1)], w=[sbk(NR1)])
            emit("vector", lambda e: e.tensor_scalar(
                t8_s[:], lg_s[:], sb(R1), None, ALU.is_ge),
                r=["lg", sbk(R1)], w=["t8"])
            emit("vector", lambda e: e.scalar_tensor_tensor(
                t8_s[:], t8_s[:], NEG, lg_s[:], ALU.mult, ALU.add),
                r=["t8", "lg"], w=["t8"])
            emit("vector", lambda e: e.reduce_max(sb(R2), t8_s[:], axis=AX),
                 r=["t8"], w=[sbk(R2)])
            emit("vector", lambda e: e.tensor_scalar(
                t8_s[:], lg_s[:], sb(R2), None, ALU.is_ge),
                r=["lg", sbk(R2)], w=["t8"])
            emit("scalar", lambda e: e.activation(
                p8_s[:], lg_s[:], AF.Exp, bias=sb(NR1), scale=1.0),
                r=["lg", sbk(NR1)], w=["p8"])
            emit("vector", lambda e: e.tensor_mul(p8_s[:], p8_s[:], t8_s[:]),
                 r=["p8", "t8"], w=["p8"])
            emit("vector", lambda e: e.reduce_sum(sb(RS), p8_s[:], axis=AX),
                 r=["p8"], w=[sbk(RS)])
            emit("vector", lambda e: e.reciprocal(sb(RSI), sb(RS)),
                 r=[sbk(RS)], w=[sbk(RSI)])
            emit("vector", lambda e, tt=tt: e.tensor_scalar(
                comb[:, tt * 8:(tt + 1) * 8], p8_s[:], sb(RSI), None, ALU.mult),
                r=["p8", sbk(RSI)], w=["comb"])
        for xp in range(NEXP // 2):
            for exh in range(2):
                ex = xp * 2 + exh
                afk = f"actfm{exh}"
                for jp in range(4):
                    pb = jp % 2
                    emit("gpsimd", lambda e, ex=ex, jp=jp, pb=pb: e.dma_start(
                        out=wgu_g[pb][:], in_=wgu_d[ex, jp, 0]),
                        w=[f"wgug{pb}"], dma=True, lat=9000, cost=350)
                    emit("gpsimd", lambda e, ex=ex, jp=jp, pb=pb: e.dma_start(
                        out=wgu_u[pb][:], in_=wgu_d[ex, jp, 1]),
                        w=[f"wguu{pb}"], dma=True, lat=9000, cost=350)
                    for tt in range(TTB):
                        pg = tt % 2
                        for a in range(NEc):
                            emit("tensor", lambda e, a=a, tt=tt, pb=pb, pg=pg: e.matmul(
                                ps_gg[pg][:, 0:256],
                                y_fm[:, a * TCB + tt * P: a * TCB + (tt + 1) * P],
                                wgu_g[pb][:, a * 256:(a + 1) * 256],
                                start=(a == 0), stop=(a == NEc - 1)),
                                r=["yfm", f"wgug{pb}"], w=[f"psgg{pg}"], cost=140)
                        for a in range(NEc):
                            emit("tensor", lambda e, a=a, tt=tt, pb=pb, pg=pg: e.matmul(
                                ps_gu[pg][:, 0:256],
                                y_fm[:, a * TCB + tt * P: a * TCB + (tt + 1) * P],
                                wgu_u[pb][:, a * 256:(a + 1) * 256],
                                start=(a == 0), stop=(a == NEc - 1)),
                                r=["yfm", f"wguu{pb}"], w=[f"psgu{pg}"], cost=140)
                        emit("scalar", lambda e, pg=pg: e.activation(
                            silu_t[:], ps_gg[pg][:, 0:256], AF.Silu),
                            r=[f"psgg{pg}"], w=["silu"], cost=580)
                        if ex < 8:
                            emit("vector", lambda e, tt=tt, ex=ex, pg=pg: e.scalar_tensor_tensor(
                                act_bt[:], silu_t[:],
                                comb[:, tt * 8 + ex: tt * 8 + ex + 1],
                                ps_gu[pg][:, 0:256], ALU.mult, ALU.mult),
                                r=["silu", "comb", f"psgu{pg}"], w=["actbt"], cost=560)
                        else:
                            emit("vector", lambda e, pg=pg: e.scalar_tensor_tensor(
                                act_bt[:], silu_t[:], 1.0,
                                ps_gu[pg][:, 0:256], ALU.mult, ALU.mult),
                                r=["silu", f"psgu{pg}"], w=["actbt"], cost=560)
                        for j in range(2):
                            emit("tensor", lambda e, j=j, pg=pg: e.transpose(
                                ps_atr[:, pg * 512 + j * P: pg * 512 + (j + 1) * P],
                                act_bt[:, j * P:(j + 1) * P], ident_s[:]),
                                r=["actbt", "ident"], w=[f"psatr{pg}"], cost=180)
                        emit("scalar", lambda e, tt=tt, jp=jp, pg=pg, exh=exh: e.copy(
                            act_fm[:, exh * (TTB * 1024) + tt * 1024 + jp * 256:
                                   exh * (TTB * 1024) + tt * 1024 + jp * 256 + 256],
                            ps_atr[:, pg * 512: pg * 512 + 256]),
                            r=[f"psatr{pg}"], w=[afk], cost=520)
            for os_ in range(4):
                for exh in range(2):
                    emit("gpsimd", lambda e, xp=xp, exh=exh, os_=os_: e.dma_start(
                        out=wd_sl[exh][:], in_=wd_d[xp * 2 + exh, os_]),
                        w=[f"wd{exh}"], dma=True, lat=9000, cost=350)
                for tt in range(TTB):
                    pd = tt % 2
                    for exh in range(2):
                        afk = f"actfm{exh}"
                        for ic in range(8):
                            emit("tensor", lambda e, ic=ic, tt=tt, exh=exh, pd=pd: e.matmul(
                                ps_d[pd][:],
                                act_fm[:, exh * (TTB * 1024) + tt * 1024 + ic * P:
                                       exh * (TTB * 1024) + tt * 1024 + (ic + 1) * P],
                                wd_sl[exh][:, ic * 512:(ic + 1) * 512],
                                start=(exh == 0 and ic == 0),
                                stop=(exh == 1 and ic == 7)),
                                r=[afk, f"wd{exh}"], w=[f"psd{pd}"], cost=270)
                    asl = accB[:, tt * E + os_ * 512: tt * E + (os_ + 1) * 512]
                    if xp == 0:
                        emit("vector", lambda e, asl=asl, pd=pd: e.tensor_copy(
                            asl, ps_d[pd][:]),
                            r=[f"psd{pd}"], w=["accB"], cost=830)
                    else:
                        emit("vector", lambda e, asl=asl, pd=pd: e.tensor_add(
                            asl, asl, ps_d[pd][:]),
                            r=[f"psd{pd}", "accB"], w=["accB"], cost=830)
        for tt in range(TTB):
            pb = tt % 2
            yck = f"yc{(t0 + tt * P) // TCA}"
            emit("sync", lambda e, tt=tt, pb=pb, t0=t0: e.dma_start(
                out=y_f32[pb][:],
                in_=y_d[t0 + tt * P: t0 + (tt + 1) * P, :]),
                r=[yck], w=[f"yf32{pb}"], dma=True, lat=2600)
            sl = y_f32[pb][:]
            asl = accB[:, tt * E:(tt + 1) * E]
            emit("vector", lambda e, sl=sl, asl=asl: e.tensor_add(sl, sl, asl),
                 r=[f"yf32{pb}", "accB"], w=[f"yf32{pb}"], cost=2400)
            emit("vector", lambda e, sl=sl: e.reduce_sum(sb(BMS), sl, axis=AX),
                 r=[f"yf32{pb}"], w=[sbk(BMS)], cost=2400)
            emit("vector", lambda e: e.tensor_scalar(
                sb(BMS), sb(BMS), 1.0 / E, None, ALU.mult),
                r=[sbk(BMS)], w=[sbk(BMS)])
            emit("vector", lambda e, sl=sl: e.tensor_scalar(
                sl, sl, sb(BMS), None, ALU.subtract),
                r=[f"yf32{pb}", sbk(BMS)], w=[f"yf32{pb}"], cost=2400)
            emit("scalar", lambda e, sl=sl, pb=pb: e.activation(
                y_tmb[pb][:], sl, AF.Square, accum_out=sb(BVS)),
                r=[f"yf32{pb}"], w=[f"ytmb{pb}", sbk(BVS)], cost=2600)
            emit("vector", lambda e: e.tensor_scalar(
                sb(BVS), sb(BVS), 1.0 / E, None, ALU.mult),
                r=[sbk(BVS)], w=[sbk(BVS)])
            emit("scalar", lambda e: e.activation(
                sb(BLT), sb(BVS), AF.Ln, bias=sb(BEPS), scale=1.0),
                r=[sbk(BVS), sbk(BEPS)], w=[sbk(BLT)])
            emit("scalar", lambda e: e.activation(
                sb(BLT), sb(BLT), AF.Exp, scale=-0.5),
                r=[sbk(BLT)], w=[sbk(BLT)])
            emit("vector", lambda e, sl=sl: e.tensor_scalar(
                sl, sl, sb(BLT), None, ALU.mult),
                r=[f"yf32{pb}", sbk(BLT)], w=[f"yf32{pb}"], cost=2400)
            emit("sync", lambda e, sl=sl, tt=tt, t0=t0, pb=pb: e.dma_start(
                out=out_d[t0 + tt * P: t0 + (tt + 1) * P, :], in_=sl),
                r=[f"yf32{pb}"], w=["outd"], dma=True, chan=f"out{pb}")

    stB.close()

    with nc.Block() as block:
        s.replay(block)
    top.close()
    return nc


def _shuffle_inputs(ins):
    f32 = np.float32
    W_in = np.ascontiguousarray(ins["L0_attn_W_in"], f32)
    wi_shuf = np.ascontiguousarray(
        W_in.reshape(NEc, P, NO, P).transpose(2, 1, 0, 3).reshape(NO, P, NEc * P))
    b_in_t = np.ascontiguousarray(np.asarray(ins["L0_attn_b_in"], f32).reshape(NO, P).T)
    conv_w_t = np.ascontiguousarray(
        np.asarray(ins["L0_attn_conv_w"], f32).reshape(NO, P, 4).transpose(1, 0, 2).reshape(P, NO * 4))
    conv_b_t = np.ascontiguousarray(np.asarray(ins["L0_attn_conv_b"], f32).reshape(NO, P).T)
    W_out = np.ascontiguousarray(ins["L0_attn_W_out"], f32)
    wo_shuf = np.ascontiguousarray(
        W_out.reshape(16, P, 4, 512).transpose(2, 1, 0, 3).reshape(4, P, 16 * 512))
    bout_bc = np.ascontiguousarray(
        np.broadcast_to(np.asarray(ins["L0_attn_b_out"], f32), (P, E)))
    wg_shuf = np.ascontiguousarray(
        np.asarray(ins["L0_ffn_w_gate"], f32).reshape(NEc, P, 8).transpose(1, 0, 2).reshape(P, NEc * 8))

    wgu = np.empty((NEXP, 4, 2, P, 16 * 256), f32)
    wgu_in = np.asarray(ins["L0_ffn_w_gate_up"], f32)
    for e in range(8):
        q = wgu_in[e].reshape(16, P, 8, 256).transpose(2, 1, 0, 3).reshape(8, P, 16 * 256)
        for jp in range(4):
            wgu[e, jp, 0] = q[jp]
            wgu[e, jp, 1] = q[4 + jp]
    qsh = np.asarray(ins["L0_ffn_w_sh_gate_up"], f32).reshape(
        16, P, 32, 256).transpose(2, 1, 0, 3).reshape(32, P, 16 * 256)
    for p2 in range(4):
        for jp in range(4):
            wgu[8 + p2, jp, 0] = qsh[p2 * 4 + jp]
            wgu[8 + p2, jp, 1] = qsh[16 + p2 * 4 + jp]

    wd = np.empty((NEXP, 4, P, 8 * 512), f32)
    wd_in = np.asarray(ins["L0_ffn_w_down"], f32)
    for e in range(8):
        wd[e] = wd_in[e].reshape(8, P, 4, 512).transpose(2, 1, 0, 3).reshape(4, P, 8 * 512)
    wshd = np.asarray(ins["L0_ffn_w_sh_down"], f32)
    for p2 in range(4):
        wd[8 + p2] = wshd[p2 * 1024:(p2 + 1) * 1024].reshape(
            8, P, 4, 512).transpose(2, 1, 0, 3).reshape(4, P, 8 * 512)

    ident = np.eye(P, dtype=ml_dtypes.bfloat16)
    swap = np.zeros((P, P), f32)
    for p in range(P):
        swap[(p + 64) % P, p] = 1.0
    tri = np.where(np.arange(P)[:, None] >= np.arange(P)[None, :],
                   0.0, NEG).astype(ml_dtypes.bfloat16)
    inv_freq = 1.0 / (10000.0 ** (np.arange(0, D, 2, dtype=np.float32) / D))
    tpos = np.arange(T, dtype=np.float32)
    freqs = np.outer(tpos, inv_freq)
    cosv = np.cos(freqs).astype(f32).T
    sinv = np.sin(freqs).astype(f32).T
    c_cos = np.ascontiguousarray(np.concatenate([cosv, cosv], 0))
    c_sin = np.ascontiguousarray(np.concatenate([-sinv, sinv], 0))

    return {
        "wi_shuf": wi_shuf, "b_in_t": b_in_t, "conv_w_t": conv_w_t,
        "conv_b_t": conv_b_t, "wo_shuf": wo_shuf, "bout_bc": bout_bc,
        "wg_shuf": wg_shuf, "wgu_shuf": wgu, "wd_shuf": wd,
        "c_ident": ident, "c_tri": tri, "c_swap": swap,
        "c_cos": c_cos, "c_sin": c_sin,
    }


def kernel(**inputs):
    shared = _shuffle_inputs(inputs)
    x = np.ascontiguousarray(np.asarray(inputs["L0_attn_x"], np.float32))
    nc = build_program()
    in_maps = []
    for b in range(x.shape[0]):
        m = dict(shared)
        m["x"] = np.ascontiguousarray(x[b, :T])
        in_maps.append(m)
    res = run_bass_kernel_spmd(nc, in_maps, core_ids=list(range(x.shape[0])))
    out = np.stack([np.asarray(r["out"]) for r in res.results], axis=0)
    return out.astype(np.float32)
